# version 17
# baseline (speedup 1.0000x reference)
import numpy as np
import ml_dtypes
from contextlib import ExitStack
import concourse.bass as bass
import concourse.mybir as mybir
from concourse.bass_utils import run_bass_kernel_spmd
from concourse.alu_op_type import AluOpType

F32 = mybir.dt.float32
BF16 = mybir.dt.bfloat16
AF = mybir.ActivationFunctionType
BF = ml_dtypes.bfloat16


def emit_ln(nc, P, rt, xnt, D=1024, T=2048):
    C, NS = D // 128, T // 512
    rt_v = rt.rearrange("(c p) t -> p c t", p=128)
    xnt_v = xnt.rearrange("(c p) t -> p c t", p=128)
    with ExitStack() as es:
        E = es.enter_context
        rts = E(nc.sbuf_tensor(P + "rts", [128, C * T], F32))
        xb = E(nc.sbuf_tensor(P + "xb", [128, 2 * T], BF16))
        sq = E(nc.sbuf_tensor(P + "sq", [128, 2 * T], BF16))
        onesc = E(nc.sbuf_tensor(P + "onesc", [128, 1], BF16))
        onesr = E(nc.sbuf_tensor(P + "onesr", [1, 128], F32))
        epsc = E(nc.sbuf_tensor(P + "epsc", [1, 1], F32))
        rows = E(nc.sbuf_tensor(P + "rows", [1, 5 * T], F32))
        rstdB = E(nc.sbuf_tensor(P + "rstdB", [128, T], F32))
        nmrB = E(nc.sbuf_tensor(P + "nmrB", [128, T], F32))
        tmp = E(nc.sbuf_tensor(P + "tmp", [128, T], F32))
        xout = E(nc.sbuf_tensor(P + "xout", [128, 2 * T], BF16))
        psA = E(nc.psum_tensor(P + "psA", [128, T], F32))
        psB = E(nc.psum_tensor(P + "psB", [128, T], F32))
        ini = E(nc.semaphore(P + "ini"))
        ld = E(nc.semaphore(P + "ld"))
        castx = E(nc.semaphore(P + "castx"))
        casts = E(nc.semaphore(P + "casts"))
        mm = E(nc.semaphore(P + "mm"))
        rw = E(nc.semaphore(P + "rw"))
        ev = E(nc.semaphore(P + "ev"))
        st0 = E(nc.semaphore(P + "st0"))
        st1 = E(nc.semaphore(P + "st1"))
        block = E(nc.Block())
        mean, ms, var, rstd, nmr = (rows[:, i * T:(i + 1) * T] for i in range(5))

        @block.gpsimd
        def _(g):
            g.memset(onesc[:, :], 1.0).then_inc(ini, 1)
            g.memset(onesr[:, :], 1.0).then_inc(ini, 1)
            g.memset(epsc[:, :], 1e-5).then_inc(ini, 1)
            for h in range(2):
                g.dma_start(
                    rts[:, h * (C // 2) * T:(h + 1) * (C // 2) * T]
                    .rearrange("p (c t) -> p c t", t=T),
                    rt_v[:, h * (C // 2):(h + 1) * (C // 2), :]).then_inc(ld, 16)
            for c in range(C):
                g.wait_ge(ev, 2 * NS + 2 * c + 2)
                g.dma_start(xnt_v[:, c, :], xout[:, (c % 2) * T:(c % 2 + 1) * T]
                            ).then_inc(st0 if c % 2 == 0 else st1, 16)
            g.wait_ge(st0, (C // 2) * 16)
            g.wait_ge(st1, (C // 2) * 16)

        @block.vector
        def _(v):
            v.wait_ge(ld, 32)
            for c in range(C):
                if c >= 2:
                    v.wait_ge(mm, c - 1)
                v.tensor_copy(xb[:, (c % 2) * T:(c % 2 + 1) * T],
                              rts[:, c * T:(c + 1) * T]).then_inc(castx, 1)
            v.wait_ge(rw, 2)
            v.tensor_tensor(var, mean, mean, op=AluOpType.mult).then_inc(rw, 1)
            v.wait_ge(rw, 3)
            v.tensor_tensor(var, ms, var, op=AluOpType.subtract).then_inc(rw, 1)
            v.wait_ge(rw, 5)
            v.reciprocal(rstd, nmr).then_inc(rw, 1)
            v.wait_ge(rw, 6)
            v.scalar_tensor_tensor(nmr, mean, -1.0, rstd,
                                   op0=AluOpType.mult, op1=AluOpType.mult).then_inc(rw, 1)
            for j in range(NS):
                v.wait_ge(mm, C + j + 1)
                v.tensor_copy(rstdB[:, j * 512:(j + 1) * 512],
                              psA[:, j * 512:(j + 1) * 512]).then_inc(ev, 1)
                v.tensor_copy(nmrB[:, j * 512:(j + 1) * 512],
                              psB[:, j * 512:(j + 1) * 512]).then_inc(ev, 1)
            for c in range(C):
                v.wait_ge(ev, 2 * NS + 2 * c)
                if c >= 2:
                    v.wait_ge(st0 if c % 2 == 0 else st1, (c // 2) * 16)
                v.tensor_tensor(tmp[:, :], rts[:, c * T:(c + 1) * T], rstdB[:, :],
                                op=AluOpType.mult).then_inc(ev, 1)
                v.wait_ge(ev, 2 * NS + 2 * c + 1)
                v.tensor_tensor(xout[:, (c % 2) * T:(c % 2 + 1) * T],
                                tmp[:, :], nmrB[:, :], op=AluOpType.add).then_inc(ev, 1)

        @block.scalar
        def _(sc):
            sc.wait_ge(ld, 32)
            sc.wait_ge(ini, 3)
            for c in range(C):
                if c >= 2:
                    sc.wait_ge(mm, c - 1)
                sc.activation(sq[:, (c % 2) * T:(c % 2 + 1) * T],
                              rts[:, c * T:(c + 1) * T], AF.Square).then_inc(casts, 1)
            sc.wait_ge(mm, C)
            sc.activation(mean, psA[0:1, :], AF.Copy, scale=1.0 / D).then_inc(rw, 1)
            sc.activation(ms, psB[0:1, :], AF.Copy, scale=1.0 / D).then_inc(rw, 1)
            sc.wait_ge(rw, 4)
            sc.activation(nmr, var, AF.Sqrt, bias=epsc[:, :]).then_inc(rw, 1)

        @block.tensor
        def _(t):
            t.wait_ge(ini, 3)
            for c in range(C):
                t.wait_ge(castx, c + 1)
                t.wait_ge(casts, c + 1)
                last = None
                for j in range(NS):
                    t.matmul(psA[0:1, j * 512:(j + 1) * 512], onesc[:, :],
                             xb[:, (c % 2) * T + j * 512:(c % 2) * T + (j + 1) * 512],
                             start=(c == 0), stop=(c == C - 1))
                    last = t.matmul(psB[0:1, j * 512:(j + 1) * 512], onesc[:, :],
                                    sq[:, (c % 2) * T + j * 512:(c % 2) * T + (j + 1) * 512],
                                    start=(c == 0), stop=(c == C - 1))
                last.then_inc(mm, 1)
            t.wait_ge(rw, 7)
            for j in range(NS):
                t.matmul(psA[:, j * 512:(j + 1) * 512], onesr[:, :],
                         rstd[:, j * 512:(j + 1) * 512], start=True, stop=True)
                t.matmul(psB[:, j * 512:(j + 1) * 512], onesr[:, :],
                         nmr[:, j * 512:(j + 1) * 512], start=True,
                         stop=True).then_inc(mm, 1)


def emit_mm(nc, P, xt, wt, out, K, O, T, colv=None, min_=None, acc=None,
            act=None, oscale=False, obias=False, out_f32=False):
    mult_in, accum_in = min_ is not None, acc is not None
    assert K % 128 == 0 and O % 512 == 0 and T % 512 == 0
    assert not (mult_in and accum_in) and not (oscale and obias)
    KC, TS = K // 128, T // 512
    OBW = 256 if K > 2048 else 512
    OB = O // OBW
    OT = OBW // 128
    OC = O // 128
    NT = OB * TS * OT
    vec_evac = mult_in or accum_in
    if vec_evac:
        assert act is None
    out_dt = F32 if out_f32 else BF16
    xt_v = xt.rearrange("(k p) t -> p k t", p=128)
    wt_v = wt.rearrange("(k p) o -> p k o", p=128)
    KG = 8
    n_xdma = (KC + KG - 1) // KG
    aux_dt = F32 if accum_in else BF16

    with ExitStack() as es:
        E = es.enter_context
        wblk = E(nc.sbuf_tensor(P + "wblk", [128, 2 * KC * OBW], BF16))
        xblk = E(nc.sbuf_tensor(P + "xblk", [128, KC * T], BF16))
        evac = E(nc.sbuf_tensor(P + "evac", [128, 2 * 512], out_dt))
        aux = E(nc.sbuf_tensor(P + "aux", [128, 2 * 512], aux_dt))
        cols = E(nc.sbuf_tensor(P + "cols", [128, max(OC, 1)], F32))
        ps = E(nc.psum_tensor(P + "ps", [128, 4 * 512], F32))
        ldw0 = E(nc.semaphore(P + "ldw0"))
        ldw1 = E(nc.semaphore(P + "ldw1"))
        lcol = E(nc.semaphore(P + "lcol"))
        lax0 = E(nc.semaphore(P + "lax0"))
        lax1 = E(nc.semaphore(P + "lax1"))
        ldx = E(nc.semaphore(P + "ldx"))
        mm = E(nc.semaphore(P + "mm"))
        ev = E(nc.semaphore(P + "ev"))
        st0 = E(nc.semaphore(P + "st0"))
        st1 = E(nc.semaphore(P + "st1"))
        block = E(nc.Block())
        WPD = max(1, (1024 // OBW))
        n_wd = (KC + WPD - 1) // WPD
        w_done = [n_wd * 16 * (obi // 2 + 1) for obi in range(OB)]

        def wsl(obi, k):
            return wblk[:, (obi % 2) * KC * OBW + k * OBW:
                        (obi % 2) * KC * OBW + (k + 1) * OBW]

        def emit_store(se, g):
            oi, it = g % OT, g // OT
            ti = it % TS
            o = (g // (TS * OT)) * OT + oi
            se.wait_ge(ev, g + 1)
            se.dma_start(out[o * 128:(o + 1) * 128, ti * 512:(ti + 1) * 512],
                         evac[:, (g % 2) * 512:(g % 2) * 512 + 512]
                         ).then_inc(st0 if g % 2 == 0 else st1, 16)

        @block.gpsimd
        def _(gp):
            if colv is not None:
                gp.dma_start(cols[:, 0:OC], colv).then_inc(lcol, 16)
            for obi in range(OB):
                if obi >= 2:
                    gp.wait_ge(mm, (obi - 1) * TS * OT)
                for k0 in range(0, KC, WPD):
                    kn = min(WPD, KC - k0)
                    gp.dma_start(
                        wblk[:, (obi % 2) * KC * OBW + k0 * OBW:
                             (obi % 2) * KC * OBW + (k0 + kn) * OBW]
                        .rearrange("p (k c) -> p k c", c=OBW),
                        wt_v[:, k0:k0 + kn, obi * OBW:(obi + 1) * OBW],
                    ).then_inc(ldw0 if obi % 2 == 0 else ldw1, 16)
        @block.sync
        def _(sync):
            for k0 in range(0, KC, KG):
                kn = min(KG, KC - k0)
                sync.dma_start(
                    xblk[:, k0 * T:(k0 + kn) * T]
                    .rearrange("p (k c) -> p k c", c=T),
                    xt_v[:, k0:k0 + kn, :],
                ).then_inc(ldx, 16)
            if mult_in or accum_in:
                src = min_ if mult_in else acc
                for g in range(NT):
                    oi, it = g % OT, g // OT
                    ti = it % TS
                    o = (g // (TS * OT)) * OT + oi
                    if g >= 2:
                        sync.wait_ge(ev, g - 1)
                    sync.dma_start(
                        aux[:, (g % 2) * 512:(g % 2) * 512 + 512],
                        src[o * 128:(o + 1) * 128,
                            ti * 512:(ti + 1) * 512]).then_inc(
                                lax0 if g % 2 == 0 else lax1, 16)

        @block.tensor
        def _(t):
            t.wait_ge(ldx, n_xdma * 16)
            for obi in range(OB):
                t.wait_ge(ldw0 if obi % 2 == 0 else ldw1, w_done[obi])
                for ti in range(TS):
                    for oi in range(OT):
                        g = (obi * TS + ti) * OT + oi
                        if g >= 4:
                            t.wait_ge(ev, g - 3)
                        pst = ps[:, (g % 4) * 512:(g % 4) * 512 + 512]
                        last = None
                        for k in range(KC):
                            last = t.matmul(
                                pst,
                                wsl(obi, k)[:, oi * 128:(oi + 1) * 128],
                                xblk[:, k * T + ti * 512:k * T + (ti + 1) * 512],
                                start=(k == 0), stop=(k == KC - 1))
                        last.then_inc(mm, 1)

        if vec_evac:
            @block.vector
            def _(v):
                op0 = AluOpType.add if obias else (
                    AluOpType.mult if oscale else AluOpType.bypass)
                op1 = AluOpType.add if accum_in else AluOpType.mult
                for g in range(NT):
                    oi = g % OT
                    o = (g // (TS * OT)) * OT + oi
                    v.wait_ge(mm, g + 1)
                    v.wait_ge(lax0 if g % 2 == 0 else lax1, (g // 2 + 1) * 16)
                    if colv is not None:
                        v.wait_ge(lcol, 16)
                    if g >= 2:
                        v.wait_ge(st0 if g % 2 == 0 else st1, (g // 2) * 16)
                    pst = ps[:, (g % 4) * 512:(g % 4) * 512 + 512]
                    ev_sl = evac[:, (g % 2) * 512:(g % 2) * 512 + 512]
                    if colv is not None:
                        v.scalar_tensor_tensor(
                            ev_sl, pst, cols[:, o:o + 1],
                            aux[:, (g % 2) * 512:(g % 2) * 512 + 512],
                            op0=op0, op1=op1).then_inc(ev, 1)
                    else:
                        v.tensor_tensor(
                            ev_sl, pst,
                            aux[:, (g % 2) * 512:(g % 2) * 512 + 512],
                            op=op1).then_inc(ev, 1)

            @block.scalar
            def _(sc):
                for g in range(NT):
                    emit_store(sc, g)
                sc.wait_ge(st0, ((NT + 1) // 2) * 16)
                sc.wait_ge(st1, (NT // 2) * 16)
        else:
            @block.scalar
            def _(sc):
                fn = act if act is not None else AF.Copy
                for g in range(NT):
                    oi = g % OT
                    o = (g // (TS * OT)) * OT + oi
                    sc.wait_ge(mm, g + 1)
                    if oscale:
                        sc.wait_ge(lcol, 16)
                    if g >= 2:
                        sc.wait_ge(st0 if g % 2 == 0 else st1, (g // 2) * 16)
                    pst = ps[:, (g % 4) * 512:(g % 4) * 512 + 512]
                    ev_sl = evac[:, (g % 2) * 512:(g % 2) * 512 + 512]
                    kw = {}
                    if oscale:
                        kw["scale"] = cols[:, o:o + 1]
                    sc.activation(ev_sl, pst, fn, **kw).then_inc(ev, 1)
                    emit_store(sc, g)
                sc.wait_ge(st0, ((NT + 1) // 2) * 16)
                sc.wait_ge(st1, (NT // 2) * 16)


def emit_vtok(nc, P, xt, wt, vp, T=2048, DH=64):
    KT = T // 128
    H = 1024 // DH
    DHp = vp.shape[3]
    hps = 512 // DH
    NW = H * KT * DHp
    with ExitStack() as es:
        E = es.enter_context
        xnts = E(nc.sbuf_tensor(P + "xnts", [128, 8 * T], BF16))
        wts = E(nc.sbuf_tensor(P + "wts", [128, 8 * 1024], BF16))
        stg = E(nc.sbuf_tensor(P + "stg", [128, 2 * 512], BF16))
        ones = (E(nc.sbuf_tensor(P + "ones", [128, NW], BF16))
                if DHp != DH else None)
        psv = E(nc.psum_tensor(P + "psv", [128, 2 * 512], F32))
        ld = E(nc.semaphore(P + "ld"))
        ldi = E(nc.semaphore(P + "ldi"))
        ldj = E(nc.semaphore(P + "ldj"))
        mm = E(nc.semaphore(P + "mm"))
        ev = E(nc.semaphore(P + "ev"))
        st0 = E(nc.semaphore(P + "st0"))
        st1 = E(nc.semaphore(P + "st1"))
        block = E(nc.Block())

        @block.gpsimd
        def _(g):
            if ones is not None:
                g.memset(ones[:, :], 1.0).then_inc(ldi, 1)
                g.wait_ge(ldi, 1)
                g.dma_start(vp.rearrange("p h k c -> p (h k c)"),
                            ones[:, :]).then_inc(ldj, 16)
            g.dma_start(xnts[:, :].rearrange("p (c t) -> p c t", t=T),
                        xt.rearrange("(c p) t -> p c t", p=128)).then_inc(ld, 16)
            g.dma_start(wts[:, :].rearrange("p (c o) -> p c o", o=1024),
                        wt.rearrange("(c p) o -> p c o", p=128)).then_inc(ld, 16)
            if ones is not None:
                g.wait_ge(ldj, 16)
            for g_i in range(2 * KT):
                kt_, j = g_i // 2, g_i % 2
                g.wait_ge(ev, g_i + 1)
                g.dma_start(vp[:, j * hps:(j + 1) * hps, kt_, 0:DH],
                            stg[:, (g_i % 2) * 512:(g_i % 2) * 512 + 512]
                            .rearrange("p (h c) -> p h c", c=DH)
                            ).then_inc(st0 if g_i % 2 == 0 else st1, 16)
            g.wait_ge(st0, KT * 16)
            g.wait_ge(st1, KT * 16)

        @block.tensor
        def _(t):
            t.wait_ge(ld, 32)
            for g_i in range(2 * KT):
                kt_, j = g_i // 2, g_i % 2
                if g_i >= 2:
                    t.wait_ge(ev, g_i - 1)
                last = None
                for c in range(8):
                    last = t.matmul(
                        psv[:, (g_i % 2) * 512:(g_i % 2) * 512 + 512],
                        xnts[:, c * T + kt_ * 128:c * T + (kt_ + 1) * 128],
                        wts[:, c * 1024 + j * 512:c * 1024 + (j + 1) * 512],
                        start=(c == 0), stop=(c == 7))
                last.then_inc(mm, 1)

        @block.vector
        def _(v):
            for g_i in range(2 * KT):
                v.wait_ge(mm, g_i + 1)
                if g_i >= 2:
                    v.wait_ge(st0 if g_i % 2 == 0 else st1, (g_i // 2) * 16)
                v.tensor_copy(stg[:, (g_i % 2) * 512:(g_i % 2) * 512 + 512],
                              psv[:, (g_i % 2) * 512:(g_i % 2) * 512 + 512]
                              ).then_inc(ev, 1)


def emit_attn0(nc, P, xnt, wqk, vp, masks, zt, T=2048, H=16, cast=None):
    KT = T // 128
    QS = T // 512
    HP = H // 2
    NQT = 2 * QS
    wqk_v = wqk.rearrange("(k p) o -> p k o", p=128)
    with ExitStack() as es:
        E = es.enter_context
        xns = E(nc.sbuf_tensor(P + "xns", [128, 8 * T], BF16))
        qb = E(nc.sbuf_tensor(P + "qb", [128, 2 * T], BF16))
        kb = E(nc.sbuf_tensor(P + "kb", [128, 2 * T], BF16))
        wqs = E(nc.sbuf_tensor(P + "wqs", [128, 2 * 2048], BF16))
        vps = E(nc.sbuf_tensor(P + "vps", [128, H * KT * 65], BF16))
        msk = E(nc.sbuf_tensor(P + "msk", [128, 4 * 512], BF16))
        expt = E(nc.sbuf_tensor(P + "expt", [128, 2 * 2 * KT * 512], BF16))
        srec = E(nc.sbuf_tensor(P + "srec", [128, 2 * 512], F32))
        recB = E(nc.sbuf_tensor(P + "recB", [128, 2 * 512], F32))
        zst = E(nc.sbuf_tensor(P + "zst", [128, 2 * 512], BF16))
        onesb = E(nc.sbuf_tensor(P + "onesb", [128, 64], F32))
        ps_s = E(nc.psum_tensor(P + "ps_s", [128, 4 * 512], F32))
        ps_z0 = E(nc.psum_tensor(P + "ps_z0", [128, 512], F32))
        ps_z1 = E(nc.psum_tensor(P + "ps_z1", [128, 512], F32))
        ps_b = E(nc.psum_tensor(P + "ps_b", [128, 512], F32))
        ps_q = E(nc.psum_tensor(P + "ps_q", [128, 512], F32))
        ini = E(nc.semaphore(P + "ini"))
        ld = E(nc.semaphore(P + "ld"))
        ldq0 = E(nc.semaphore(P + "ldq0"))
        ldq1 = E(nc.semaphore(P + "ldq1"))
        s_qk = E(nc.semaphore(P + "s_qk"))
        s_qe = E(nc.semaphore(P + "s_qe"))
        s_mm = E(nc.semaphore(P + "s_mm"))
        s_exp = E(nc.semaphore(P + "s_exp"))
        s_msk = E(nc.semaphore(P + "s_msk"))
        s_av = E(nc.semaphore(P + "s_av"))
        s_rc = E(nc.semaphore(P + "s_rc"))
        s_bc = E(nc.semaphore(P + "s_bc"))
        s_ev = E(nc.semaphore(P + "s_ev"))
        st0 = E(nc.semaphore(P + "st0"))
        st1 = E(nc.semaphore(P + "st1"))
        s_wc = E(nc.semaphore(P + "s_wc"))
        block = E(nc.Block())

        groups = [(hp, qs) for hp in range(HP) for qs in range(QS)]
        NG0 = len(groups)
        NCAST = NG0
        exp_base, msk_base, mm_base = [], [], []
        e = m = s = 0
        for (hp, qs) in groups:
            exp_base.append(e); msk_base.append(m); mm_base.append(s)
            e += 4 * qs + 4
            m += 8
            s += 2 * (4 * qs + 4)
        mm_total = s

        def mmb(gidx):
            return mm_base[gidx] if gidx < NG0 else mm_total

        @block.gpsimd
        def _(g):
            g.memset(onesb[:, :], 1.0).then_inc(ini, 1)
            g.dma_start(xns[:, :].rearrange("p (c t) -> p c t", t=T),
                        xnt.rearrange("(c p) t -> p c t", p=128)).then_inc(ld, 16)
            g.dma_start(vps[:, :].rearrange("p (h k c) -> p h k c", k=KT, c=65),
                        vp[:, :, :, :]).then_inc(ld, 16)
            g.dma_start(msk[:, :].rearrange("p (j c) -> p j c", c=512),
                        masks[:, :, :]).then_inc(ld, 16)
            def wload(hp):
                par = hp % 2
                if hp >= 2:
                    g.wait_ge(s_qk, NQT * (hp - 1))
                g.dma_start(wqs[:, par * 2048:par * 2048 + 1024]
                            .rearrange("p (k c) -> p k c", c=128),
                            wqk_v[:, :, 128 * hp:128 * hp + 128]).then_inc(
                                ldq0 if par == 0 else ldq1, 16)
                g.dma_start(wqs[:, par * 2048 + 1024:par * 2048 + 2048]
                            .rearrange("p (k c) -> p k c", c=128),
                            wqk_v[:, :, 1024 + 128 * hp:1024 + 128 * hp + 128]
                            ).then_inc(ldq0 if par == 0 else ldq1, 16)

            def wcast(i):
                if cast is None:
                    return
                dst, csrc = cast
                rows = 16384 // NCAST
                g.dma_start(dst[i * rows:(i + 1) * rows, :],
                            csrc[i * rows:(i + 1) * rows, :]).then_inc(s_wc, 16)

            wload(0)
            if HP > 1:
                wload(1)
            for g_i, (hp, qs) in enumerate(groups):
                if qs == 0 and hp + 2 < HP:
                    wload(hp + 2)
                wcast(g_i)
                ep = (g_i % 2) * 2 * KT * 512
                for j in range(4):
                    k = 4 * qs + j
                    g.wait_ge(s_exp, exp_base[g_i] + k + 1)
                    for hh in range(2):
                        g.tensor_tensor(expt[:, ep + (2 * k + hh) * 512:
                                             ep + (2 * k + hh + 1) * 512],
                                        expt[:, ep + (2 * k + hh) * 512:
                                             ep + (2 * k + hh + 1) * 512],
                                        msk[:, j * 512:(j + 1) * 512],
                                        op=AluOpType.mult).then_inc(s_msk, 1)
                for hh in range(2):
                    g.wait_ge(s_ev, 4 * g_i + 2 * hh + 2)
                    g.dma_start(zt[64 * (2 * hp + hh):64 * (2 * hp + hh) + 64,
                                   qs * 512:(qs + 1) * 512],
                                zst[0:64, hh * 512:hh * 512 + 512]
                                ).then_inc(st0 if hh == 0 else st1, 16)
            g.wait_ge(st0, NG0 * 16)
            g.wait_ge(st1, NG0 * 16)
            if cast is not None:
                g.wait_ge(s_wc, NCAST * 16)

        @block.vector
        def _(v):
            v.wait_ge(ld, 48)
            v.memset(vps[:, :].rearrange("p (g c) -> p g c", c=65)[:, :, 64:65],
                     1.0).then_inc(ini, 1)

            def qk_evacs(hp):
                par = hp % 2
                if hp >= 2:
                    v.wait_ge(s_mm, mmb(QS * (hp - 1)))
                for j in range(NQT):
                    ii = hp * NQT + j
                    v.wait_ge(s_qk, ii + 1)
                    dst = qb if j < QS else kb
                    tsl = j % QS
                    v.tensor_copy(dst[:, par * T + tsl * 512:par * T + (tsl + 1) * 512],
                                  ps_q[:, :]).then_inc(s_qe, 1)

            qk_evacs(0)
            for g_i, (hp, qs) in enumerate(groups):
                if qs == 0 and hp + 1 < HP:
                    qk_evacs(hp + 1)
                for hh in range(2):
                    psz = ps_z0 if hh == 0 else ps_z1
                    v.wait_ge(s_av, 2 * g_i + hh + 1)
                    if g_i > 0 or hh > 0:
                        v.wait_ge(s_bc, 2 * g_i + hh)
                    v.reciprocal(srec[64:65, hh * 512:hh * 512 + 512],
                                 psz[64:65, :]).then_inc(s_rc, 1)
                    v.wait_ge(s_bc, 2 * g_i + hh + 1)
                    v.tensor_copy(recB[0:64, hh * 512:hh * 512 + 512],
                                  ps_b[0:64, :]).then_inc(s_ev, 1)
                    if g_i >= 1:
                        v.wait_ge(st0 if hh == 0 else st1, g_i * 16)
                    v.wait_ge(s_ev, 4 * g_i + 2 * hh + 1)
                    v.tensor_tensor(zst[0:64, hh * 512:hh * 512 + 512],
                                    psz[0:64, :],
                                    recB[0:64, hh * 512:hh * 512 + 512],
                                    op=AluOpType.mult).then_inc(s_ev, 1)

        @block.tensor
        def _(t):
            t.wait_ge(ld, 48)
            t.wait_ge(ini, 2)

            def qk_mms(hp):
                par = hp % 2
                out = []
                for j in range(NQT):
                    def f(hp=hp, j=j, par=par):
                        ii = hp * NQT + j
                        t.wait_ge(ldq0 if par == 0 else ldq1,
                                  (hp // 2 + 1) * 32)
                        if ii > 0:
                            t.wait_ge(s_qe, ii)
                        tsl = j % QS
                        wof = par * 2048 + (0 if j < QS else 1024)
                        last = None
                        for c in range(8):
                            last = t.matmul(
                                ps_q[:, :],
                                wqs[:, wof + c * 128:wof + (c + 1) * 128],
                                xns[:, c * T + tsl * 512:c * T + (tsl + 1) * 512],
                                start=(c == 0), stop=(c == 7))
                        last.then_inc(s_qk, 1)
                    out.append(f)
                return out

            for f in qk_mms(0):
                f()

            def a_work(g):
                hp, qs = groups[g]
                ktmax = 4 * qs + 4
                ep = (g % 2) * 2 * KT * 512
                work = []
                for hh in range(2):
                    h = 2 * hp + hh
                    psz = ps_z0 if hh == 0 else ps_z1

                    def pre(g=g, hh=hh):
                        if g > 0:
                            t.wait_ge(s_ev, 4 * (g - 1) + 2 * hh + 2)
                    work.append(pre)
                    for k in range(ktmax):
                        def f(g=g, hh=hh, k=k, psz=psz, h=h, qs=qs,
                              ktmax=ktmax, ep=ep):
                            t.wait_ge(s_exp, exp_base[g] + k + 1)
                            if k >= 4 * qs:
                                t.wait_ge(s_msk, msk_base[g] +
                                          2 * (k - 4 * qs) + hh + 1)
                            t.matmul(psz[0:65, :],
                                     vps[:, (h * KT + k) * 65:(h * KT + k + 1) * 65],
                                     expt[:, ep + (2 * k + hh) * 512:
                                          ep + (2 * k + hh + 1) * 512],
                                     start=(k == 0), stop=(k == ktmax - 1)
                                     ).then_inc(s_av, 1 if k == ktmax - 1 else 0)
                        work.append(f)

                    def bc(g=g, hh=hh):
                        t.wait_ge(s_rc, 2 * g + hh + 1)
                        w = 4 * g + 3 * hh - 1
                        if w > 0:
                            t.wait_ge(s_ev, w)
                        t.matmul(ps_b[0:64, :],
                                 onesb[64:65, 0:64],
                                 srec[64:65, hh * 512:hh * 512 + 512],
                                 start=True, stop=True).then_inc(s_bc, 1)
                    work.append(bc)
                return work

            pend = []
            qkq = []
            for g_i in range(NG0 + 1):
                if g_i < NG0:
                    hp, qs = groups[g_i]
                    if qs == 0 and hp + 1 < HP:
                        qkq = qk_mms(hp + 1)
                    ktmax = 4 * qs + 4
                    hc = (hp % 2) * T
                    nper = max(1, (len(pend) + ktmax - 1) // ktmax)
                    nqk = max(1, (len(qkq) + 4 * ktmax - 1) // (4 * ktmax)) if qkq else 0
                    t.wait_ge(s_qe, NQT * (hp + 1))
                    for k in range(ktmax):
                        if exp_base[g_i] + k >= 2:
                            t.wait_ge(s_exp, exp_base[g_i] + k - 1)
                        for hh in range(2):
                            r0 = 64 * hh
                            t.matmul(ps_s[:, (2 * (k % 2) + hh) * 512:
                                          (2 * (k % 2) + hh) * 512 + 512],
                                     kb[r0:r0 + 64, hc + k * 128:hc + (k + 1) * 128],
                                     qb[r0:r0 + 64, hc + qs * 512:hc + (qs + 1) * 512],
                                     start=True, stop=True,
                                     tile_position=(r0, 0)).then_inc(s_mm, 1)
                        for _ in range(nper):
                            if pend:
                                pend.pop(0)()
                    for f in qkq:
                        f()
                    qkq = []
                for f in pend:
                    f()
                pend = a_work(g_i) if g_i < NG0 else []

        @block.scalar
        def _(sc):
            sc.wait_ge(ini, 2)
            for g_i, (hp, qs) in enumerate(groups):
                ktmax = 4 * qs + 4
                ep = (g_i % 2) * 2 * KT * 512
                for k in range(ktmax):
                    sc.wait_ge(s_mm, mm_base[g_i] + 2 * k + 2)
                    if k == 0 and g_i >= 2:
                        sc.wait_ge(s_av, 2 * (g_i - 1))
                    sc.activation(expt[:, ep + 2 * k * 512:ep + (2 * k + 2) * 512],
                                  ps_s[:, 2 * (k % 2) * 512:(2 * (k % 2) + 2) * 512],
                                  AF.Exp, scale=0.125).then_inc(s_exp, 1)


def emit_attn1(nc, P, qt, kt, vh, dw, zt, T=2048, H=8):
    KT = T // 128
    QS = T // 512
    with ExitStack() as es:
        E = es.enter_context
        qts = E(nc.sbuf_tensor(P + "qts", [128, 8 * T], BF16))
        kts = E(nc.sbuf_tensor(P + "kts", [128, 8 * T], BF16))
        vhs = E(nc.sbuf_tensor(P + "vhs", [128, H * KT * 128], BF16))
        expt = E(nc.sbuf_tensor(P + "expt", [128, 2 * 2 * KT * 512], BF16))
        srec = E(nc.sbuf_tensor(P + "srec", [1, 2 * 512], F32))
        recB = E(nc.sbuf_tensor(P + "recB", [128, 2 * 512], F32))
        tmp0 = E(nc.sbuf_tensor(P + "tmp0", [128, 512], F32))
        tmp1 = E(nc.sbuf_tensor(P + "tmp1", [128, 512], F32))
        zst = E(nc.sbuf_tensor(P + "zst", [128, 2 * 512], BF16))
        onesc = E(nc.sbuf_tensor(P + "onesc", [128, 1], BF16))
        onesr = E(nc.sbuf_tensor(P + "onesr", [1, 128], F32))
        ndr = E(nc.sbuf_tensor(P + "ndr", [1, 128], F32))
        dwsb = E(nc.sbuf_tensor(P + "dwsb", [1, 1], F32))
        ps_s = E(nc.psum_tensor(P + "ps_s", [128, 4 * 512], F32))
        ps_sum = E(nc.psum_tensor(P + "ps_sum", [128, 2 * 512], F32))
        ps_z0 = E(nc.psum_tensor(P + "ps_z0", [128, 512], F32))
        ps_z1 = E(nc.psum_tensor(P + "ps_z1", [128, 512], F32))
        ini = E(nc.semaphore(P + "ini"))
        ld = E(nc.semaphore(P + "ld"))
        s_mm = E(nc.semaphore(P + "s_mm"))
        s_exp = E(nc.semaphore(P + "s_exp"))
        s_av = E(nc.semaphore(P + "s_av"))
        s_rc = E(nc.semaphore(P + "s_rc"))
        s_bc = E(nc.semaphore(P + "s_bc"))
        s_ev = E(nc.semaphore(P + "s_ev"))
        s_nd = E(nc.semaphore(P + "s_nd"))
        st0 = E(nc.semaphore(P + "st0"))
        st1 = E(nc.semaphore(P + "st1"))
        block = E(nc.Block())
        groups = [(h, qs) for h in range(H) for qs in range(QS)]
        NG = len(groups)

        @block.gpsimd
        def _(g):
            g.memset(onesc[:, :], 1.0).then_inc(ini, 1)
            g.memset(onesr[:, :], 1.0).then_inc(ini, 1)
            g.dma_start(qts[:, :].rearrange("p (c t) -> p c t", t=T),
                        qt.rearrange("(c p) t -> p c t", p=128)).then_inc(ld, 16)
            g.dma_start(kts[:, :].rearrange("p (c t) -> p c t", t=T),
                        kt.rearrange("(c p) t -> p c t", p=128)).then_inc(ld, 16)
            g.dma_start(vhs[:, :].rearrange("p (h k c) -> p h k c", k=KT, c=128),
                        vh[:, :, :, :]).then_inc(ld, 16)
            g.dma_start(dwsb[:, :], dw).then_inc(ld, 16)
            for g_i, (h, qs) in enumerate(groups):
                g.wait_ge(s_ev, 5 * g_i + 5)
                g.dma_start(zt[128 * h:128 * h + 128, qs * 512:(qs + 1) * 512],
                            zst[:, (g_i % 2) * 512:(g_i % 2) * 512 + 512]
                            ).then_inc(st0 if g_i % 2 == 0 else st1, 16)
            g.wait_ge(st0, ((NG + 1) // 2) * 16)
            g.wait_ge(st1, (NG // 2) * 16)

        @block.tensor
        def _(t):
            t.wait_ge(ld, 64)
            t.wait_ge(ini, 2)
            t.matmul(ps_z1[0:1, 0:128], dwsb[:, :], onesr[:, :],
                     start=True, stop=True).then_inc(s_nd, 1)

            def a_work(g):
                h, qs = groups[g]
                ep = (g % 2) * 2 * KT * 512
                eb = KT * g
                work = []

                def pre():
                    if g > 0:
                        t.wait_ge(s_ev, 5 * g - 1)
                        t.wait_ge(s_rc, 2 * g)
                    if g == 0:
                        t.wait_ge(s_nd, 2)
                work.append(pre)
                for i in range(2):
                    psz = ps_z0 if i == 0 else ps_z1
                    for k in range(KT):
                        def f(i=i, k=k, psz=psz):
                            t.wait_ge(s_exp, eb + k + 1)
                            t.matmul(ps_sum[0:1, i * 512:i * 512 + 512],
                                     onesc[:, :],
                                     expt[:, ep + (2 * k + i) * 512:
                                          ep + (2 * k + i + 1) * 512],
                                     start=(k == 0), stop=(k == KT - 1))
                            t.matmul(psz[:, :],
                                     vhs[:, (h * KT + k) * 128:(h * KT + k + 1) * 128],
                                     expt[:, ep + (2 * k + i) * 512:
                                          ep + (2 * k + i + 1) * 512],
                                     start=(k == 0), stop=(k == KT - 1)
                                     ).then_inc(s_av, 1 if k == KT - 1 else 0)
                        work.append(f)

                def bc():
                    t.wait_ge(s_rc, 2 * g + 2)
                    t.matmul(ps_sum[:, 0:512], onesr[:, :], srec[:, 0:512],
                             start=True, stop=True).then_inc(s_bc, 1)
                    t.matmul(ps_sum[:, 512:1024], ndr[:, :], srec[:, 512:1024],
                             start=True, stop=True).then_inc(s_bc, 1)
                work.append(bc)
                return work

            pend = []
            for g_i in range(NG + 1):
                if g_i < NG:
                    h, qs = groups[g_i]
                    ep = (g_i % 2) * 2 * KT * 512
                    eb = KT * g_i
                    nper = max(1, (len(pend) + KT - 1) // KT)
                    for k in range(KT):
                        if g_i + k > 0:
                            t.wait_ge(s_exp, eb + k - 1)
                        for i in range(2):
                            r0 = i * 64
                            t.matmul(ps_s[:, (2 * (k % 2) + i) * 512:
                                          (2 * (k % 2) + i) * 512 + 512],
                                     kts[r0:r0 + 64, h * T + k * 128:
                                         h * T + (k + 1) * 128],
                                     qts[r0:r0 + 64, h * T + qs * 512:
                                         h * T + (qs + 1) * 512],
                                     start=True, stop=True,
                                     tile_position=(r0, 0)).then_inc(s_mm, 1)
                        for _ in range(nper):
                            if pend:
                                pend.pop(0)()
                for f in pend:
                    f()
                pend = a_work(g_i) if g_i < NG else []

        @block.scalar
        def _(sc):
            sc.wait_ge(s_nd, 1)
            sc.activation(ndr[:, :], ps_z1[0:1, 0:128], AF.Copy,
                          scale=-1.0).then_inc(s_nd, 1)
            for g_i, (h, qs) in enumerate(groups):
                ebm = 2 * KT * g_i
                ep = (g_i % 2) * 2 * KT * 512
                for k in range(KT):
                    sc.wait_ge(s_mm, ebm + 2 * k + 2)
                    if k == 0 and g_i >= 2:
                        sc.wait_ge(s_av, 2 * (g_i - 1))
                    sc.activation(expt[:, ep + 2 * k * 512:ep + (2 * k + 2) * 512],
                                  ps_s[:, 2 * (k % 2) * 512:(2 * (k % 2) + 2) * 512],
                                  AF.Exp, scale=0.125).then_inc(s_exp, 1)

        @block.vector
        def _(v):
            for g_i, (h, qs) in enumerate(groups):
                v.wait_ge(s_av, 2 * g_i + 1)
                if g_i > 0:
                    v.wait_ge(s_bc, 2 * g_i)
                v.reciprocal(srec[:, 0:512], ps_sum[0:1, 0:512]).then_inc(s_rc, 1)
                v.wait_ge(s_av, 2 * g_i + 2)
                v.reciprocal(srec[:, 512:1024], ps_sum[0:1, 512:1024]).then_inc(s_rc, 1)
                v.wait_ge(s_bc, 2 * g_i + 1)
                v.tensor_copy(recB[:, 0:512], ps_sum[:, 0:512]).then_inc(s_ev, 1)
                v.wait_ge(s_bc, 2 * g_i + 2)
                v.tensor_copy(recB[:, 512:1024], ps_sum[:, 512:1024]).then_inc(s_ev, 1)
                v.wait_ge(s_ev, 5 * g_i + 1)
                v.tensor_tensor(tmp0[:, :], ps_z0[:, :], recB[:, 0:512],
                                op=AluOpType.mult).then_inc(s_ev, 1)
                v.wait_ge(s_ev, 5 * g_i + 2)
                v.tensor_tensor(tmp1[:, :], ps_z1[:, :], recB[:, 512:1024],
                                op=AluOpType.mult).then_inc(s_ev, 1)
                v.wait_ge(s_ev, 5 * g_i + 4)
                if g_i >= 2:
                    v.wait_ge(st0 if g_i % 2 == 0 else st1, (g_i // 2) * 16)
                v.tensor_tensor(zst[:, (g_i % 2) * 512:(g_i % 2) * 512 + 512],
                                tmp0[:, :], tmp1[:, :],
                                op=AluOpType.add).then_inc(s_ev, 1)


def emit_rms(nc, P, ztin, nw, bo, dwin, nt, bec_out, T=2048, D=1024):
    C, NS = D // 128, T // 512
    EPS = float(np.finfo(np.float32).eps)
    with ExitStack() as es:
        E = es.enter_context
        zts = E(nc.sbuf_tensor(P + "zts", [128, C * T], BF16))
        sqb = E(nc.sbuf_tensor(P + "sqb", [128, 2 * T], BF16))
        nwc = E(nc.sbuf_tensor(P + "nwc", [128, C], F32))
        boc = E(nc.sbuf_tensor(P + "boc", [128, C], F32))
        bec = E(nc.sbuf_tensor(P + "bec", [128, C], F32))
        dwsb = E(nc.sbuf_tensor(P + "dwsb", [1, 1], F32))
        odw = E(nc.sbuf_tensor(P + "odw", [1, 1], F32))
        odwB = E(nc.sbuf_tensor(P + "odwB", [128, 1], F32))
        epsc = E(nc.sbuf_tensor(P + "epsc", [1, 1], F32))
        thr3 = E(nc.sbuf_tensor(P + "thr3", [1, 1], F32))
        er3 = E(nc.sbuf_tensor(P + "er3", [1, 1], F32))
        onesc = E(nc.sbuf_tensor(P + "onesc", [128, 1], BF16))
        onesr = E(nc.sbuf_tensor(P + "onesr", [1, 128], F32))
        rows = E(nc.sbuf_tensor(P + "rows", [1, 2 * T], F32))
        rstdB = E(nc.sbuf_tensor(P + "rstdB", [128, T], F32))
        xout = E(nc.sbuf_tensor(P + "xout", [128, 2 * T], BF16))
        psA = E(nc.psum_tensor(P + "psA", [128, T], F32))
        psB = E(nc.psum_tensor(P + "psB", [128, T], F32))
        ini = E(nc.semaphore(P + "ini"))
        epse = E(nc.semaphore(P + "epse"))
        ld = E(nc.semaphore(P + "ld"))
        casts = E(nc.semaphore(P + "casts"))
        mm = E(nc.semaphore(P + "mm"))
        rw = E(nc.semaphore(P + "rw"))
        ev = E(nc.semaphore(P + "ev"))
        st0 = E(nc.semaphore(P + "st0"))
        st1 = E(nc.semaphore(P + "st1"))
        block = E(nc.Block())
        row_s, row_r = rows[:, 0:T], rows[:, T:2 * T]

        @block.gpsimd
        def _(g):
            g.memset(onesc[:, :], 1.0).then_inc(ini, 1)
            g.memset(onesr[:, :], 1.0).then_inc(ini, 1)
            g.memset(thr3[:, :], 3.0).then_inc(ini, 1)
            g.dma_start(zts[:, :].rearrange("p (c t) -> p c t", t=T),
                        ztin.rearrange("(c p) t -> p c t", p=128)).then_inc(ld, 16)
            g.dma_start(nwc[:, :], nw).then_inc(ld, 16)
            g.dma_start(boc[:, :], bo).then_inc(ld, 16)
            g.dma_start(dwsb[:, :], dwin).then_inc(ld, 16)
            for c in range(C):
                g.wait_ge(ev, NS + 2 + c + 1)
                g.dma_start(nt.rearrange("(c p) t -> p c t", p=128)[:, c, :],
                            xout[:, (c % 2) * T:(c % 2 + 1) * T]
                            ).then_inc(st0 if c % 2 == 0 else st1, 16)
            g.wait_ge(ev, NS + 2)
            g.dma_start(bec_out, bec[:, :]).then_inc(st0, 16)
            g.wait_ge(st0, (C // 2 + 1) * 16)
            g.wait_ge(st1, (C // 2) * 16)

        @block.scalar
        def _(sc):
            sc.wait_ge(ld, 64)
            sc.wait_ge(ini, 3)
            for c in range(C):
                if c >= 2:
                    sc.wait_ge(mm, c - 1)
                sc.activation(sqb[:, (c % 2) * T:(c % 2 + 1) * T],
                              zts[:, c * T:(c + 1) * T], AF.Square).then_inc(casts, 1)
            sc.wait_ge(epse, 3)
            sc.activation(epsc[:, :], epsc[:, :], AF.Abs).then_inc(epse, 1)
            sc.wait_ge(mm, C)
            sc.wait_ge(epse, 4)
            sc.activation(row_s, psA[0:1, :], AF.Sqrt, scale=1.0 / D,
                          bias=epsc[:, :]).then_inc(rw, 1)
            sc.activation(odw[:, :], dwsb[:, :], AF.Copy, scale=-1.0,
                          bias=1.0).then_inc(rw, 1)

        @block.vector
        def _(v):
            v.wait_ge(ini, 3)
            v.reciprocal(er3[:, :], thr3[:, :]).then_inc(epse, 1)
            v.wait_ge(epse, 1)
            v.tensor_scalar(epsc[:, :], er3[:, :], 4.0, -1.0,
                            op0=AluOpType.mult, op1=AluOpType.add).then_inc(epse, 1)
            v.wait_ge(epse, 2)
            v.tensor_scalar(epsc[:, :], epsc[:, :], 3.0, -1.0,
                            op0=AluOpType.mult, op1=AluOpType.add).then_inc(epse, 1)
            v.wait_ge(rw, 1)
            v.reciprocal(row_r, row_s).then_inc(rw, 1)
            v.wait_ge(rw, 3)
            v.tensor_scalar_mul(row_r, row_r, odw[:, :]).then_inc(rw, 1)
            for j in range(NS):
                v.wait_ge(mm, C + j + 1)
                v.tensor_copy(rstdB[:, j * 512:(j + 1) * 512],
                              psB[:, j * 512:(j + 1) * 512]).then_inc(ev, 1)
            v.wait_ge(mm, C + NS + 1)
            v.tensor_copy(odwB[:, :], psA[:, 0:1]).then_inc(ev, 1)
            v.wait_ge(ev, NS + 1)
            v.tensor_scalar_mul(bec[:, :], boc[:, :], odwB[:, :]).then_inc(ev, 1)
            for c in range(C):
                v.wait_ge(ev, NS + 2 + c)
                if c >= 2:
                    v.wait_ge(st0 if c % 2 == 0 else st1, (c // 2) * 16)
                v.scalar_tensor_tensor(xout[:, (c % 2) * T:(c % 2 + 1) * T],
                                       zts[:, c * T:(c + 1) * T], nwc[:, c:c + 1],
                                       rstdB[:, :], op0=AluOpType.mult,
                                       op1=AluOpType.mult).then_inc(ev, 1)

        @block.tensor
        def _(t):
            t.wait_ge(ini, 1)
            for c in range(C):
                t.wait_ge(casts, c + 1)
                last = None
                for j in range(NS):
                    last = t.matmul(psA[0:1, j * 512:(j + 1) * 512], onesc[:, :],
                                    sqb[:, (c % 2) * T + j * 512:(c % 2) * T + (j + 1) * 512],
                                    start=(c == 0), stop=(c == C - 1))
                last.then_inc(mm, 1)
            t.wait_ge(rw, 4)
            for j in range(NS):
                t.matmul(psB[:, j * 512:(j + 1) * 512], onesr[:, :],
                         row_r[:, j * 512:(j + 1) * 512], start=True,
                         stop=True).then_inc(mm, 1)
            t.wait_ge(rw, 4)
            t.matmul(psA[:, 0:1], onesr[:, :], odw[:, :], start=True,
                     stop=True).then_inc(mm, 1)


def emit_rails(nc, P, xnt, rmc, ruc, gz, T=2048):
    gz_v = gz.rearrange("(r p) t -> p r t", p=128)
    NS = T // 512
    with ExitStack() as es:
        E = es.enter_context
        xnts = E(nc.sbuf_tensor(P + "xnts", [128, 8 * T], BF16))
        rmcs = E(nc.sbuf_tensor(P + "rmcs", [128, 32 * 128], BF16))
        rucs = E(nc.sbuf_tensor(P + "rucs", [128, 128 * 128], BF16))
        ybuf = E(nc.sbuf_tensor(P + "ybuf", [128, 2 * T], BF16))
        zbuf = E(nc.sbuf_tensor(P + "zbuf", [128, 2 * T], BF16))
        ps_y = E(nc.psum_tensor(P + "ps_y", [128, 2 * 512], F32))
        ps_z = E(nc.psum_tensor(P + "ps_z", [128, 2 * 512], F32))
        ld = E(nc.semaphore(P + "ld"))
        s_ym = E(nc.semaphore(P + "s_ym"))
        s_ye = E(nc.semaphore(P + "s_ye"))
        s_zm = E(nc.semaphore(P + "s_zm"))
        s_ze = E(nc.semaphore(P + "s_ze"))
        st0 = E(nc.semaphore(P + "st0"))
        st1 = E(nc.semaphore(P + "st1"))
        block = E(nc.Block())

        @block.gpsimd
        def _(g):
            g.dma_start(xnts[:, :].rearrange("p (c t) -> p c t", t=T),
                        xnt.rearrange("(c p) t -> p c t", p=128)).then_inc(ld, 16)
            for i in range(4):
                g.dma_start(rmcs[:, i * 8 * 128:(i + 1) * 8 * 128]
                            .rearrange("p (g c) -> p g c", c=128),
                            rmc[:, i * 8:(i + 1) * 8, :]).then_inc(ld, 16)
            for i in range(16):
                g.dma_start(rucs[:, i * 8 * 128:(i + 1) * 8 * 128]
                            .rearrange("p (g c) -> p g c", c=128),
                            ruc[:, i * 8:(i + 1) * 8, :]).then_inc(ld, 16)
            for zg in range(128):
                g.wait_ge(s_ze, NS * zg + NS)
                g.dma_start(gz_v[:, zg, :],
                            zbuf[:, (zg % 2) * T:(zg % 2 + 1) * T]
                            ).then_inc(st0 if zg % 2 == 0 else st1, 16)
            g.wait_ge(st0, 64 * 16)
            g.wait_ge(st1, 64 * 16)

        @block.tensor
        def _(t):
            t.wait_ge(ld, 21 * 16)
            for gi in range(32):
                ro = gi % 8
                for s in range(NS):
                    ys = gi * NS + s
                    if ys >= 2:
                        t.wait_ge(s_ye, ys - 1)
                    t.matmul(ps_y[:, (s % 2) * 512:(s % 2) * 512 + 512],
                             rmcs[:, gi * 128:(gi + 1) * 128],
                             xnts[:, ro * T + s * 512:ro * T + (s + 1) * 512],
                             start=True, stop=True).then_inc(s_ym, 1)
                for zc in range(4):
                    for s in range(NS):
                        zi = (gi * 4 + zc) * NS + s
                        t.wait_ge(s_ye, gi * NS + s + 1)
                        if zi >= 2:
                            t.wait_ge(s_ze, zi - 1)
                        t.matmul(ps_z[:, (zi % 2) * 512:(zi % 2) * 512 + 512],
                                 rucs[:, (gi * 4 + zc) * 128:(gi * 4 + zc + 1) * 128],
                                 ybuf[:, (gi % 2) * T + s * 512:(gi % 2) * T + (s + 1) * 512],
                                 start=True, stop=True).then_inc(s_zm, 1)

        @block.scalar
        def _(sc):
            for gi in range(32):
                for s in range(NS):
                    ys = gi * NS + s
                    sc.wait_ge(s_ym, ys + 1)
                    if gi >= 2:
                        sc.wait_ge(s_zm, 4 * NS * (gi - 1))
                    sc.activation(ybuf[:, (gi % 2) * T + s * 512:(gi % 2) * T + (s + 1) * 512],
                                  ps_y[:, (s % 2) * 512:(s % 2) * 512 + 512],
                                  AF.Copy).then_inc(s_ye, 1)
                for zc in range(4):
                    zg = gi * 4 + zc
                    for s in range(NS):
                        zi = (gi * 4 + zc) * NS + s
                        sc.wait_ge(s_zm, zi + 1)
                        if zg >= 2 and s == 0:
                            sc.wait_ge(st0 if zg % 2 == 0 else st1,
                                       (zg // 2) * 16)
                        sc.activation(zbuf[:, (zg % 2) * T + s * 512:(zg % 2) * T + (s + 1) * 512],
                                      ps_z[:, (zi % 2) * 512:(zi % 2) * 512 + 512],
                                      AF.Gelu).then_inc(s_ze, 1)


def emit_cproj(nc, P, gzin, wct, acc, out, T=2048, K=16384, O=1024):
    KC = K // 128
    NS = T // 512
    OC = O // 128
    gz_v = gzin.rearrange("(k p) t -> p k t", p=128)
    wct_v = wct.rearrange("(k p) o -> p k o", p=128)
    acc_v = acc.rearrange("(c p) t -> p c t", p=128)
    out_v = out.rearrange("(c p) t -> p c t", p=128)
    KG = 8
    NWD = KC // KG
    with ExitStack() as es:
        E = es.enter_context
        gzs = E(nc.sbuf_tensor(P + "gzs", [128, KC * 512], BF16))
        wcts = E(nc.sbuf_tensor(P + "wcts", [128, 2 * KG * O], BF16))
        accs = E(nc.sbuf_tensor(P + "accs", [128, OC * 512], F32))
        outb = E(nc.sbuf_tensor(P + "outb", [128, OC * 512], F32))
        ps = E(nc.psum_tensor(P + "ps", [128, OC * 512], F32))
        ldx = E(nc.semaphore(P + "ldx"))
        lda = E(nc.semaphore(P + "lda"))
        ldw = E(nc.semaphore(P + "ldw"))
        s_mm = E(nc.semaphore(P + "s_mm"))
        s_ev = E(nc.semaphore(P + "s_ev"))
        st = E(nc.semaphore(P + "st"))
        block = E(nc.Block())

        @block.sync
        def _(sy):
            for ts in range(NS):
                if ts > 0:
                    sy.wait_ge(s_mm, NWD * ts)
                for k0 in range(0, KC, KG):
                    sy.dma_start(gzs[:, k0 * 512:(k0 + KG) * 512]
                                 .rearrange("p (k t) -> p k t", t=512),
                                 gz_v[:, k0:k0 + KG, ts * 512:(ts + 1) * 512]
                                 ).then_inc(ldx, 16)

        @block.gpsimd
        def _(g):
            for ts in range(NS):
                if ts > 0:
                    g.wait_ge(s_ev, OC * ts)
                g.dma_start(accs[:, :].rearrange("p (c t) -> p c t", t=512),
                            acc_v[:, :, ts * 512:(ts + 1) * 512]).then_inc(lda, 16)
                for wi in range(NWD):
                    g.wait_ge(s_mm, max(0, ts * NWD + wi - 1))
                    g.dma_start(wcts[:, (wi % 2) * KG * O:((wi % 2) + 1) * KG * O]
                                .rearrange("p (k o) -> p k o", o=O),
                                wct_v[:, wi * KG:(wi + 1) * KG, :]).then_inc(ldw, 16)

        @block.tensor
        def _(t):
            for ts in range(NS):
                t.wait_ge(ldx, (ts + 1) * NWD * 16)
                if ts > 0:
                    t.wait_ge(s_ev, OC * ts)
                for wi in range(NWD):
                    t.wait_ge(ldw, (ts * NWD + wi + 1) * 16)
                    last = None
                    for kc in range(KG):
                        k = wi * KG + kc
                        for ot in range(OC):
                            last = t.matmul(
                                ps[:, ot * 512:(ot + 1) * 512],
                                wcts[:, (wi % 2) * KG * O + kc * O + ot * 128:
                                     (wi % 2) * KG * O + kc * O + (ot + 1) * 128],
                                gzs[:, k * 512:(k + 1) * 512],
                                start=(k == 0), stop=(k == KC - 1))
                    last.then_inc(s_mm, 1)

        @block.vector
        def _(v):
            for ts in range(NS):
                v.wait_ge(s_mm, (ts + 1) * NWD)
                v.wait_ge(lda, (ts + 1) * 16)
                if ts > 0:
                    v.wait_ge(st, ts * OC * 16)
                for ot in range(OC):
                    v.tensor_tensor(outb[:, ot * 512:(ot + 1) * 512],
                                    ps[:, ot * 512:(ot + 1) * 512],
                                    accs[:, ot * 512:(ot + 1) * 512],
                                    op=AluOpType.add).then_inc(s_ev, 1)

        @block.scalar
        def _(sc):
            for ts in range(NS):
                for ot in range(OC):
                    sc.wait_ge(s_ev, ts * OC + ot + 1)
                    sc.dma_start(out_v[:, ot, ts * 512:(ts + 1) * 512],
                                 outb[:, ot * 512:(ot + 1) * 512]).then_inc(st, 16)
            sc.wait_ge(st, NS * OC * 16)


def emit_railsproj(nc, P, xnt, rmc, ruc, wct, acc, out, T=2048):
    NS = T // 512
    K = 16384
    KC = K // 128
    wct_v = wct.rearrange("(k p) o -> p k o", p=128)
    acc_v = acc.rearrange("(c p) t -> p c t", p=128)
    out_v = out.rearrange("(c p) t -> p c t", p=128)
    xnt_v = xnt.rearrange("(c p) t -> p c t", p=128)
    WG = 4

    with ExitStack() as es:
        E = es.enter_context
        xsl = E(nc.sbuf_tensor(P + "xsl", [128, 8 * 512], BF16))
        rmcs = E(nc.sbuf_tensor(P + "rmcs", [128, 32 * 128], BF16))
        rucb = E(nc.sbuf_tensor(P + "rucb", [128, 2 * 32 * 128], BF16))
        ybuf = E(nc.sbuf_tensor(P + "ybuf", [128, 2 * 512], BF16))
        gzs = E(nc.sbuf_tensor(P + "gzs", [128, KC * 512], BF16))
        wcts = E(nc.sbuf_tensor(P + "wcts", [128, 4 * WG * 1024], BF16))
        accs = E(nc.sbuf_tensor(P + "accs", [128, 2 * 512], F32))
        oev = E(nc.sbuf_tensor(P + "oev", [128, 2 * 512], F32))
        ps = E(nc.psum_tensor(P + "ps", [128, 8 * 512], F32))
        ldx = E(nc.semaphore(P + "ldx"))
        ldr0 = E(nc.semaphore(P + "ldr0"))
        ldr1 = E(nc.semaphore(P + "ldr1"))
        ldm = E(nc.semaphore(P + "ldm"))
        ldw0 = E(nc.semaphore(P + "ldw0"))
        ldw1 = E(nc.semaphore(P + "ldw1"))
        ldw2 = E(nc.semaphore(P + "ldw2"))
        ldw3 = E(nc.semaphore(P + "ldw3"))
        lda0 = E(nc.semaphore(P + "lda0"))
        lda1 = E(nc.semaphore(P + "lda1"))
        s_ym = E(nc.semaphore(P + "s_ym"))
        s_ye = E(nc.semaphore(P + "s_ye"))
        s_zm = E(nc.semaphore(P + "s_zm"))
        s_ze = E(nc.semaphore(P + "s_ze"))
        s_cm = E(nc.semaphore(P + "s_cm"))
        s_cv = E(nc.semaphore(P + "s_cv"))
        st0 = E(nc.semaphore(P + "st0"))
        st1 = E(nc.semaphore(P + "st1"))
        block = E(nc.Block())

        @block.sync
        def _(sy):
            for ts in range(NS):
                for wi in range(0, KC // WG, 2):
                    gwi = (KC // WG) * ts + wi
                    if gwi >= 4:
                        sy.wait_ge(s_cm, WG * gwi - 3 * WG)
                    sy.dma_start(wcts[:, (gwi % 4) * WG * 1024:((gwi % 4) + 1) * WG * 1024]
                                 .rearrange("p (k o) -> p k o", o=1024),
                                 wct_v[:, wi * WG:(wi + 1) * WG, :]).then_inc(
                                     [ldw0, ldw1, ldw2, ldw3][gwi % 4], 16)

        @block.gpsimd
        def _(g):
            g.dma_start(rmcs[:, :].rearrange("p (g c) -> p g c", c=128),
                        rmc[:, :, :]).then_inc(ldm, 16)
            for ts in range(NS):
                for j in range(4):
                    jj = ts * 4 + j
                    if jj >= 2:
                        g.wait_ge(s_zm, 128 * ts + 32 * (j - 1) if j >= 1
                                  else 128 * (ts - 1) + 32 * 3)
                    g.dma_start(rucb[:, (jj % 2) * 32 * 128:((jj % 2) + 1) * 32 * 128]
                                .rearrange("p (g c) -> p g c", c=128),
                                ruc[:, j * 32:(j + 1) * 32, :]).then_inc(
                                    ldr0 if jj % 2 == 0 else ldr1, 16)
                for wi in range(1, KC // WG, 2):
                    gwi = (KC // WG) * ts + wi
                    if gwi >= 4:
                        g.wait_ge(s_cm, WG * gwi - 3 * WG)
                    g.dma_start(wcts[:, (gwi % 4) * WG * 1024:((gwi % 4) + 1) * WG * 1024]
                                .rearrange("p (k o) -> p k o", o=1024),
                                wct_v[:, wi * WG:(wi + 1) * WG, :]).then_inc(
                                    [ldw0, ldw1, ldw2, ldw3][gwi % 4], 16)
                for ot in range(8):
                    cv = 8 * ts + ot
                    if cv >= 2:
                        g.wait_ge(s_cv, cv - 1)
                    g.dma_start(accs[:, (cv % 2) * 512:(cv % 2) * 512 + 512],
                                acc_v[:, ot, ts * 512:(ts + 1) * 512]).then_inc(
                                    lda0 if cv % 2 == 0 else lda1, 16)

        @block.tensor
        def _(t):
            t.wait_ge(ldm, 16)
            for ts in range(NS):
                t.wait_ge(ldx, (ts + 1) * 16)
                if ts > 0:
                    t.wait_ge(s_cv, 8 * ts)
                xo = 0
                for gi in range(32):
                    ro = gi % 8
                    yi = 32 * ts + gi
                    if yi >= 2:
                        t.wait_ge(s_ye, yi - 1)
                    t.matmul(ps[:, (gi % 2) * 512:(gi % 2) * 512 + 512],
                             rmcs[:, gi * 128:(gi + 1) * 128],
                             xsl[:, xo + ro * 512:xo + (ro + 1) * 512],
                             start=True, stop=True).then_inc(s_ym, 1)
                    for zc in range(4):
                        zl = gi * 4 + zc
                        zi = 128 * ts + zl
                        jj = ts * 4 + zl // 32
                        t.wait_ge(ldr0 if jj % 2 == 0 else ldr1,
                                  (jj // 2 + 1) * 16)
                        t.wait_ge(s_ye, yi + 1)
                        if zi >= 2:
                            t.wait_ge(s_ze, zi - 1)
                        t.matmul(ps[:, (2 + zl % 2) * 512:(2 + zl % 2) * 512 + 512],
                                 rucb[:, ((jj % 2) * 32 + zl % 32) * 128:
                                      ((jj % 2) * 32 + zl % 32 + 1) * 128],
                                 ybuf[:, (gi % 2) * 512:(gi % 2) * 512 + 512],
                                 start=True, stop=True).then_inc(s_zm, 1)
                t.wait_ge(s_ze, 128 * (ts + 1))
                t.wait_ge(s_ye, 32 * (ts + 1))
                for zc in range(KC):
                    wi = zc // WG
                    gwi = (KC // WG) * ts + wi
                    t.wait_ge([ldw0, ldw1, ldw2, ldw3][gwi % 4],
                              (gwi // 4 + 1) * 16)
                    wof = (gwi % 4) * WG * 1024 + (zc % WG) * 1024
                    last = None
                    for ot in range(8):
                        last = t.matmul(
                            ps[:, ot * 512:(ot + 1) * 512],
                            wcts[:, wof + ot * 128:wof + (ot + 1) * 128],
                            gzs[:, zc * 512:(zc + 1) * 512],
                            start=(zc == 0), stop=(zc == KC - 1))
                    last.then_inc(s_cm, 1)

        @block.vector
        def _(v):
            for ts in range(NS):
                for gi in range(32):
                    yi = 32 * ts + gi
                    v.wait_ge(s_ym, yi + 1)
                    if yi >= 2:
                        v.wait_ge(s_zm, 4 * (yi - 1))
                    v.tensor_copy(ybuf[:, (gi % 2) * 512:(gi % 2) * 512 + 512],
                                  ps[:, (gi % 2) * 512:(gi % 2) * 512 + 512]
                                  ).then_inc(s_ye, 1)
                v.wait_ge(s_cm, 128 * (ts + 1))
                for ot in range(8):
                    cv = 8 * ts + ot
                    v.wait_ge(lda0 if cv % 2 == 0 else lda1, (cv // 2 + 1) * 16)
                    if cv >= 2:
                        v.wait_ge(st0 if cv % 2 == 0 else st1, (cv // 2) * 16)
                    v.tensor_tensor(oev[:, (cv % 2) * 512:(cv % 2) * 512 + 512],
                                    ps[:, ot * 512:(ot + 1) * 512],
                                    accs[:, (cv % 2) * 512:(cv % 2) * 512 + 512],
                                    op=AluOpType.add).then_inc(s_cv, 1)

        @block.scalar
        def _(sc):
            sc.dma_start(xsl[:, :].rearrange("p (c t) -> p c t", t=512),
                         xnt_v[:, :, 0:512]).then_inc(ldx, 16)
            for ts in range(NS):
                for zl in range(128):
                    zi = 128 * ts + zl
                    sc.wait_ge(s_zm, zi + 1)
                    if ts > 0 and zl == 0:
                        sc.wait_ge(s_cm, 128 * ts)
                    sc.activation(gzs[:, zl * 512:(zl + 1) * 512],
                                  ps[:, (2 + zl % 2) * 512:(2 + zl % 2) * 512 + 512],
                                  AF.Gelu).then_inc(s_ze, 1)
                if ts + 1 < NS:
                    sc.wait_ge(s_ym, 32 * (ts + 1))
                    sc.dma_start(xsl[:, :].rearrange("p (c t) -> p c t", t=512),
                                 xnt_v[:, :, (ts + 1) * 512:(ts + 2) * 512]
                                 ).then_inc(ldx, 16)
                for ot in range(8):
                    cv = 8 * ts + ot
                    sc.wait_ge(s_cv, cv + 1)
                    sc.dma_start(out_v[:, ot, ts * 512:(ts + 1) * 512],
                                 oev[:, (cv % 2) * 512:(cv % 2) * 512 + 512]
                                 ).then_inc(st0 if cv % 2 == 0 else st1, 16)
            sc.wait_ge(st0, NS * 4 * 16)
            sc.wait_ge(st1, NS * 4 * 16)


def emit_glu(nc, P, xt, w1, w2, s1c, s2c, out, T=2048, K=1024, O=4096):
    KC, TS = K // 128, T // 512
    OB = O // 512
    OC = O // 128
    NT = OB * TS * 4
    xt_v = xt.rearrange("(k p) t -> p k t", p=128)
    w1_v = w1.rearrange("(k p) o -> p k o", p=128)
    w2_v = w2.rearrange("(k p) o -> p k o", p=128)
    with ExitStack() as es:
        E = es.enter_context
        xblk = E(nc.sbuf_tensor(P + "xblk", [128, KC * T], BF16))
        wb1 = E(nc.sbuf_tensor(P + "wb1", [128, 2 * KC * 512], BF16))
        wb2 = E(nc.sbuf_tensor(P + "wb2", [128, 2 * KC * 512], BF16))
        tmp = E(nc.sbuf_tensor(P + "tmp", [128, 2 * 512], BF16))
        evac = E(nc.sbuf_tensor(P + "evac", [128, 2 * 512], BF16))
        col1 = E(nc.sbuf_tensor(P + "col1", [128, OC], F32))
        col2 = E(nc.sbuf_tensor(P + "col2", [128, OC], F32))
        ps = E(nc.psum_tensor(P + "ps", [128, 4 * 512], F32))
        ldx = E(nc.semaphore(P + "ldx"))
        ldw0 = E(nc.semaphore(P + "ldw0"))
        ldw1 = E(nc.semaphore(P + "ldw1"))
        lcol = E(nc.semaphore(P + "lcol"))
        m1 = E(nc.semaphore(P + "m1"))
        mm = E(nc.semaphore(P + "mm"))
        sil = E(nc.semaphore(P + "sil"))
        ev = E(nc.semaphore(P + "ev"))
        st0 = E(nc.semaphore(P + "st0"))
        st1 = E(nc.semaphore(P + "st1"))
        block = E(nc.Block())

        @block.gpsimd
        def _(gp):
            gp.dma_start(col1[:, :], s1c).then_inc(lcol, 16)
            gp.dma_start(col2[:, :], s2c).then_inc(lcol, 16)
            for obi in range(OB):
                if obi >= 2:
                    gp.wait_ge(mm, (obi - 1) * TS * 4)
                po = (obi % 2) * KC * 512
                gp.dma_start(wb1[:, po:po + KC * 512]
                             .rearrange("p (k c) -> p k c", c=512),
                             w1_v[:, :, obi * 512:(obi + 1) * 512]).then_inc(
                                 ldw0 if obi % 2 == 0 else ldw1, 16)
                gp.dma_start(wb2[:, po:po + KC * 512]
                             .rearrange("p (k c) -> p k c", c=512),
                             w2_v[:, :, obi * 512:(obi + 1) * 512]).then_inc(
                                 ldw0 if obi % 2 == 0 else ldw1, 16)

        @block.sync
        def _(sy):
            sy.dma_start(xblk[:, :].rearrange("p (k c) -> p k c", c=T),
                         xt_v[:, :, :]).then_inc(ldx, 16)

        @block.tensor
        def _(t):
            t.wait_ge(ldx, 16)
            for obi in range(OB):
                t.wait_ge(ldw0 if obi % 2 == 0 else ldw1,
                          (obi // 2 + 1) * 32)
                po = (obi % 2) * KC * 512
                for ti in range(TS):
                    for oi in range(4):
                        g = (obi * TS + ti) * 4 + oi
                        b = (g % 2) * 2
                        if g >= 2:
                            t.wait_ge(sil, g - 1)
                            t.wait_ge(ev, g - 1)
                        last = None
                        for k in range(KC):
                            last = t.matmul(
                                ps[:, b * 512:(b + 1) * 512],
                                wb1[:, po + k * 512 + oi * 128:
                                    po + k * 512 + (oi + 1) * 128],
                                xblk[:, k * T + ti * 512:k * T + (ti + 1) * 512],
                                start=(k == 0), stop=(k == KC - 1))
                        last.then_inc(m1, 1)
                        for k in range(KC):
                            last = t.matmul(
                                ps[:, (b + 1) * 512:(b + 2) * 512],
                                wb2[:, po + k * 512 + oi * 128:
                                    po + k * 512 + (oi + 1) * 128],
                                xblk[:, k * T + ti * 512:k * T + (ti + 1) * 512],
                                start=(k == 0), stop=(k == KC - 1))
                        last.then_inc(mm, 1)

        @block.scalar
        def _(sc):
            sc.wait_ge(lcol, 32)
            for g in range(NT):
                o = (g // (TS * 4)) * 4 + g % 4
                b = (g % 2) * 2
                sc.wait_ge(m1, g + 1)
                if g >= 2:
                    sc.wait_ge(ev, g - 1)
                sc.activation(tmp[:, (g % 2) * 512:(g % 2) * 512 + 512],
                              ps[:, b * 512:(b + 1) * 512], AF.Silu,
                              scale=col1[:, o:o + 1]).then_inc(sil, 1)
                sc.wait_ge(ev, g + 1)
                oi, it = g % 4, g // 4
                ti = it % TS
                sc.dma_start(out[o * 128:(o + 1) * 128, ti * 512:(ti + 1) * 512],
                             evac[:, (g % 2) * 512:(g % 2) * 512 + 512]
                             ).then_inc(st0 if g % 2 == 0 else st1, 16)
            sc.wait_ge(st0, ((NT + 1) // 2) * 16)
            sc.wait_ge(st1, (NT // 2) * 16)

        @block.vector
        def _(v):
            for g in range(NT):
                o = (g // (TS * 4)) * 4 + g % 4
                b = (g % 2) * 2
                v.wait_ge(mm, g + 1)
                v.wait_ge(sil, g + 1)
                if g >= 2:
                    v.wait_ge(st0 if g % 2 == 0 else st1, (g // 2) * 16)
                v.scalar_tensor_tensor(
                    evac[:, (g % 2) * 512:(g % 2) * 512 + 512],
                    ps[:, (b + 1) * 512:(b + 2) * 512], col2[:, o:o + 1],
                    tmp[:, (g % 2) * 512:(g % 2) * 512 + 512],
                    op0=AluOpType.mult, op1=AluOpType.mult).then_inc(ev, 1)


def build_fused(T=2048):
    nc = bass.Bass("TRN2", target_bir_lowering=False, debug=False)
    ei = lambda n, s, d: nc.dram_tensor(n, s, d, kind="ExternalInput").ap()
    sc = lambda n, s, d: nc.dram_tensor(n, s, d).ap()

    rt = ei("rt", [1024, T], F32)
    qkw0 = ei("qkw0", [1024, 2048], F32)
    wv0 = ei("wv0", [1024, 1024], F32)
    wo0 = ei("wo0", [1024, 1024], F32)
    bo0 = ei("bo0", [128, 8], F32)
    masks = ei("masks", [128, 4, 512], BF16)
    rmc = ei("rmc", [128, 32, 128], F32)
    ruc = ei("ruc", [128, 128, 128], F32)
    wct = ei("wct", [16384, 1024], F32)
    qkw1 = ei("qkw1", [1024, 2048], F32)
    wv1 = ei("wv1", [1024, 1024], F32)
    wo1 = ei("wo1", [1024, 1024], F32)
    nw1 = ei("nw1", [128, 8], F32)
    bo1 = ei("bo1", [128, 8], F32)
    dw = ei("dw", [1, 1], F32)
    w1t = ei("w1t", [1024, 4096], F32)
    w2t = ei("w2t", [1024, 4096], F32)
    h1t = ei("h1t", [4096, 4096], F32)
    h2t = ei("h2t", [4096, 4096], F32)
    w3t = ei("w3t", [4096, 1024], F32)
    s1c = ei("s1c", [128, 32], F32)
    s2c = ei("s2c", [128, 32], F32)
    hs1c = ei("hs1c", [128, 32], F32)
    hs2c = ei("hs2c", [128, 32], F32)
    s3c = ei("s3c", [128, 8], F32)

    xnt = sc("xnt", [1024, T], BF16)
    qk = sc("qk", [2048, T], BF16)
    vp = sc("vp", [128, 16, T // 128, 65], BF16)
    vh = sc("vh", [128, 8, T // 128, 128], BF16)
    zts = sc("ztsc", [1024, T], BF16)
    wctb = sc("wctb", [16384, 1024], BF16)
    rta = sc("rta", [1024, T], F32)
    gz = sc("gz", [16384, T], BF16)
    rt1 = sc("rt1", [1024, T], F32)
    ntc = sc("ntc", [1024, T], BF16)
    bec = sc("bec", [128, 8], F32)
    rtb = sc("rtb", [1024, T], F32)
    aT = sc("aT", [4096, T], BF16)
    g1 = sc("g1", [4096, T], BF16)
    g2 = sc("g2", [4096, T], BF16)
    g3 = sc("g3", [4096, T], BF16)
    out = nc.dram_tensor("out", [1024, T], F32, kind="ExternalOutput").ap()

    phases = [
        lambda: emit_ln(nc, "a_", rt, xnt, 1024, T),
        lambda: emit_vtok(nc, "c_", xnt, wv0, vp, T, 64),
        lambda: emit_attn0(nc, "d_", xnt, qkw0, vp, masks, zts, T, 16,
                           cast=(wctb, wct)),
        lambda: emit_mm(nc, "e_", zts, wo0, rta, 1024, 1024, T, colv=bo0,
                        acc=rt, obias=True, out_f32=True),
        lambda: emit_railsproj(nc, "f_", xnt, rmc, ruc, wctb, rta, rt1, T),
        lambda: emit_ln(nc, "h_", rt1, xnt, 1024, T),
        lambda: emit_mm(nc, "i_", xnt, qkw1, qk, 1024, 2048, T),
        lambda: emit_vtok(nc, "j_", xnt, wv1, vh, T, 128),
        lambda: emit_attn1(nc, "k_", qk[0:1024, :], qk[1024:2048, :], vh,
                           dw[:, :], zts, T, 8),
        lambda: emit_rms(nc, "l_", zts, nw1[:, :], bo1[:, :], dw[:, :], ntc,
                         bec[:, :], T, 1024),
        lambda: emit_mm(nc, "m_", ntc, wo1, rtb, 1024, 1024, T, colv=bec[:, :],
                        acc=rt1, obias=True, out_f32=True),
        lambda: emit_glu(nc, "n_", xnt, w1t, w2t, s1c[:, :], s2c[:, :], g1, T),
        lambda: emit_mm(nc, "p_", g1, h1t, g2, 4096, 4096, T, colv=hs1c[:, :],
                        act=AF.Gelu, oscale=True),
        lambda: emit_mm(nc, "q_", g2, h2t, g3, 4096, 4096, T, colv=hs2c[:, :],
                        act=AF.Gelu, oscale=True),
        lambda: emit_mm(nc, "r_", g3, w3t, out, 4096, 1024, T, colv=s3c[:, :],
                        acc=rtb, oscale=True, out_f32=True),
    ]
    for ph in phases:
        with nc.reset_on_exit():
            ph()
    return nc


def host_inputs(inp):
    f32 = np.float32
    C = np.ascontiguousarray
    A = np.asarray
    qkw0 = C(np.concatenate([A(inp["L0_attn_Wq"]).T, A(inp["L0_attn_Wk"]).T],
                            axis=1)).astype(f32)
    wv0 = C(A(inp["L0_attn_Wv"]).T.astype(f32))
    wo0 = C(A(inp["L0_attn_Wo"]).T.astype(f32))
    bo0 = C(A(inp["L0_attn_bo"]).astype(f32).reshape(8, 128).T)
    rm = A(inp["L0_ffn_rail_maker"]).astype(f32)
    ru = A(inp["L0_ffn_rail_up_proj"]).astype(f32)
    rmc = np.zeros((128, 32, 128), f32)
    ruc = np.zeros((128, 128, 128), f32)
    for h in range(4):
        for ro in range(8):
            gi = h * 8 + ro
            for rr in range(8):
                rmc[16*rr:16*rr+16, gi, 16*rr:16*rr+16] = rm[h, 8*ro+rr]
            for zc in range(4):
                for d in range(2):
                    ruc[16*(2*zc+d):16*(2*zc+d)+16, gi*4+zc, 64*d:64*d+64] = \
                        ru[h, 8*ro+2*zc+d].T
    wct = C(A(inp["L0_ffn_w_cproj"]).T.astype(f32))
    qkw1 = C(np.concatenate([A(inp["L1_attn_Wq"]).T, A(inp["L1_attn_Wk"]).T],
                            axis=1)).astype(f32)
    wv1 = C(A(inp["L1_attn_Wv"]).T.astype(f32))
    wo1 = C(A(inp["L1_attn_Wo"]).T.astype(f32))
    nw1 = C(A(inp["L1_attn_norm_w"]).astype(f32).reshape(8, 128).T)
    bo1 = C(A(inp["L1_attn_bo"]).astype(f32).reshape(8, 128).T)
    dw = A(inp["L1_attn_diff_weight"]).astype(f32).reshape(1, 1)
    w1t = C(A(inp["L1_ffn_w1_w"]).T.astype(f32))
    w2t = C(A(inp["L1_ffn_w2_w"]).T.astype(f32))
    h1t = C(A(inp["L1_ffn_h1_w"]).T.astype(f32))
    h2t = C(A(inp["L1_ffn_h2_w"]).T.astype(f32))
    w3t = C(A(inp["L1_ffn_w3_w"]).T.astype(f32))
    s1c = C(A(inp["L1_ffn_w1_s"]).astype(f32).reshape(32, 128).T)
    s2c = C(A(inp["L1_ffn_w2_s"]).astype(f32).reshape(32, 128).T)
    hs1c = C(A(inp["L1_ffn_h1_s"]).astype(f32).reshape(32, 128).T)
    hs2c = C(A(inp["L1_ffn_h2_s"]).astype(f32).reshape(32, 128).T)
    s3c = C(A(inp["L1_ffn_w3_s"]).astype(f32).reshape(8, 128).T)
    r = np.arange(128)[:, None]
    c = np.arange(512)[None, :]
    m = np.zeros((128, 4, 512), f32)
    for j in range(4):
        blk = c // 128
        m[:, j, :] = ((blk > j) | ((blk == j) & (r <= c % 128))).astype(f32)
    masks = m.astype(BF)
    shared = dict(qkw0=qkw0, wv0=wv0, wo0=wo0, bo0=bo0, masks=masks, rmc=rmc,
                  ruc=ruc, wct=wct, qkw1=qkw1, wv1=wv1, wo1=wo1, nw1=nw1,
                  bo1=bo1, dw=dw, w1t=w1t, w2t=w2t, h1t=h1t, h2t=h2t, w3t=w3t,
                  s1c=s1c, s2c=s2c, hs1c=hs1c, hs2c=hs2c, s3c=s3c)
    x = A(inp["L0_attn_batch"]).astype(f32)
    maps = []
    for cc in range(x.shape[0]):
        m2 = dict(shared)
        m2["rt"] = C(x[cc].T)
        maps.append(m2)
    return maps


def kernel(**inp):
    maps = host_inputs(inp)
    nc = build_fused(2048)
    res = run_bass_kernel_spmd(nc, maps, core_ids=list(range(len(maps))))
    return np.stack([r["out"].T for r in res.results], axis=0)


# version 18
# speedup vs baseline: 1.0220x; 1.0220x over previous
import numpy as np
import ml_dtypes
from contextlib import ExitStack
import concourse.bass as bass
import concourse.mybir as mybir
from concourse.bass_utils import run_bass_kernel_spmd
from concourse.alu_op_type import AluOpType

F32 = mybir.dt.float32
BF16 = mybir.dt.bfloat16
AF = mybir.ActivationFunctionType
BF = ml_dtypes.bfloat16


def emit_ln(nc, P, rt, xnt, D=1024, T=2048):
    C, NS = D // 128, T // 512
    rt_v = rt.rearrange("(c p) t -> p c t", p=128)
    xnt_v = xnt.rearrange("(c p) t -> p c t", p=128)
    with ExitStack() as es:
        E = es.enter_context
        rts = E(nc.sbuf_tensor(P + "rts", [128, C * T], F32))
        xb = E(nc.sbuf_tensor(P + "xb", [128, 2 * T], BF16))
        sq = E(nc.sbuf_tensor(P + "sq", [128, 2 * T], BF16))
        onesc = E(nc.sbuf_tensor(P + "onesc", [128, 1], BF16))
        onesr = E(nc.sbuf_tensor(P + "onesr", [1, 128], F32))
        epsc = E(nc.sbuf_tensor(P + "epsc", [1, 1], F32))
        rows = E(nc.sbuf_tensor(P + "rows", [1, 5 * T], F32))
        rstdB = E(nc.sbuf_tensor(P + "rstdB", [128, T], F32))
        nmrB = E(nc.sbuf_tensor(P + "nmrB", [128, T], F32))
        tmp = E(nc.sbuf_tensor(P + "tmp", [128, T], F32))
        xout = E(nc.sbuf_tensor(P + "xout", [128, 2 * T], BF16))
        psA = E(nc.psum_tensor(P + "psA", [128, T], F32))
        psB = E(nc.psum_tensor(P + "psB", [128, T], F32))
        ini = E(nc.semaphore(P + "ini"))
        ld = E(nc.semaphore(P + "ld"))
        castx = E(nc.semaphore(P + "castx"))
        casts = E(nc.semaphore(P + "casts"))
        mm = E(nc.semaphore(P + "mm"))
        rw = E(nc.semaphore(P + "rw"))
        ev = E(nc.semaphore(P + "ev"))
        st0 = E(nc.semaphore(P + "st0"))
        st1 = E(nc.semaphore(P + "st1"))
        block = E(nc.Block())
        mean, ms, var, rstd, nmr = (rows[:, i * T:(i + 1) * T] for i in range(5))

        @block.gpsimd
        def _(g):
            g.memset(onesc[:, :], 1.0).then_inc(ini, 1)
            g.memset(onesr[:, :], 1.0).then_inc(ini, 1)
            g.memset(epsc[:, :], 1e-5).then_inc(ini, 1)
            for h in range(2):
                g.dma_start(
                    rts[:, h * (C // 2) * T:(h + 1) * (C // 2) * T]
                    .rearrange("p (c t) -> p c t", t=T),
                    rt_v[:, h * (C // 2):(h + 1) * (C // 2), :]).then_inc(ld, 16)
            for c in range(C):
                g.wait_ge(ev, 2 * NS + 2 * c + 2)
                g.dma_start(xnt_v[:, c, :], xout[:, (c % 2) * T:(c % 2 + 1) * T]
                            ).then_inc(st0 if c % 2 == 0 else st1, 16)
            g.wait_ge(st0, (C // 2) * 16)
            g.wait_ge(st1, (C // 2) * 16)

        @block.vector
        def _(v):
            v.wait_ge(ld, 32)
            for c in range(C):
                if c >= 2:
                    v.wait_ge(mm, c - 1)
                v.tensor_copy(xb[:, (c % 2) * T:(c % 2 + 1) * T],
                              rts[:, c * T:(c + 1) * T]).then_inc(castx, 1)
            v.wait_ge(rw, 2)
            v.tensor_tensor(var, mean, mean, op=AluOpType.mult).then_inc(rw, 1)
            v.wait_ge(rw, 3)
            v.tensor_tensor(var, ms, var, op=AluOpType.subtract).then_inc(rw, 1)
            v.wait_ge(rw, 5)
            v.reciprocal(rstd, nmr).then_inc(rw, 1)
            v.wait_ge(rw, 6)
            v.scalar_tensor_tensor(nmr, mean, -1.0, rstd,
                                   op0=AluOpType.mult, op1=AluOpType.mult).then_inc(rw, 1)
            for j in range(NS):
                v.wait_ge(mm, C + j + 1)
                v.tensor_copy(rstdB[:, j * 512:(j + 1) * 512],
                              psA[:, j * 512:(j + 1) * 512]).then_inc(ev, 1)
                v.tensor_copy(nmrB[:, j * 512:(j + 1) * 512],
                              psB[:, j * 512:(j + 1) * 512]).then_inc(ev, 1)
            for c in range(C):
                v.wait_ge(ev, 2 * NS + 2 * c)
                if c >= 2:
                    v.wait_ge(st0 if c % 2 == 0 else st1, (c // 2) * 16)
                v.tensor_tensor(tmp[:, :], rts[:, c * T:(c + 1) * T], rstdB[:, :],
                                op=AluOpType.mult).then_inc(ev, 1)
                v.wait_ge(ev, 2 * NS + 2 * c + 1)
                v.tensor_tensor(xout[:, (c % 2) * T:(c % 2 + 1) * T],
                                tmp[:, :], nmrB[:, :], op=AluOpType.add).then_inc(ev, 1)

        @block.scalar
        def _(sc):
            sc.wait_ge(ld, 32)
            sc.wait_ge(ini, 3)
            for c in range(C):
                if c >= 2:
                    sc.wait_ge(mm, c - 1)
                sc.activation(sq[:, (c % 2) * T:(c % 2 + 1) * T],
                              rts[:, c * T:(c + 1) * T], AF.Square).then_inc(casts, 1)
            sc.wait_ge(mm, C)
            sc.activation(mean, psA[0:1, :], AF.Copy, scale=1.0 / D).then_inc(rw, 1)
            sc.activation(ms, psB[0:1, :], AF.Copy, scale=1.0 / D).then_inc(rw, 1)
            sc.wait_ge(rw, 4)
            sc.activation(nmr, var, AF.Sqrt, bias=epsc[:, :]).then_inc(rw, 1)

        @block.tensor
        def _(t):
            t.wait_ge(ini, 3)
            for c in range(C):
                t.wait_ge(castx, c + 1)
                t.wait_ge(casts, c + 1)
                last = None
                for j in range(NS):
                    t.matmul(psA[0:1, j * 512:(j + 1) * 512], onesc[:, :],
                             xb[:, (c % 2) * T + j * 512:(c % 2) * T + (j + 1) * 512],
                             start=(c == 0), stop=(c == C - 1))
                    last = t.matmul(psB[0:1, j * 512:(j + 1) * 512], onesc[:, :],
                                    sq[:, (c % 2) * T + j * 512:(c % 2) * T + (j + 1) * 512],
                                    start=(c == 0), stop=(c == C - 1))
                last.then_inc(mm, 1)
            t.wait_ge(rw, 7)
            for j in range(NS):
                t.matmul(psA[:, j * 512:(j + 1) * 512], onesr[:, :],
                         rstd[:, j * 512:(j + 1) * 512], start=True, stop=True)
                t.matmul(psB[:, j * 512:(j + 1) * 512], onesr[:, :],
                         nmr[:, j * 512:(j + 1) * 512], start=True,
                         stop=True).then_inc(mm, 1)


def emit_mm(nc, P, xt, wt, out, K, O, T, colv=None, min_=None, acc=None,
            act=None, oscale=False, obias=False, out_f32=False):
    mult_in, accum_in = min_ is not None, acc is not None
    assert K % 128 == 0 and O % 512 == 0 and T % 512 == 0
    assert not (mult_in and accum_in) and not (oscale and obias)
    KC, TS = K // 128, T // 512
    OBW = 256 if K > 2048 else 512
    OB = O // OBW
    OT = OBW // 128
    OC = O // 128
    NT = OB * TS * OT
    vec_evac = mult_in or accum_in
    if vec_evac:
        assert act is None
    out_dt = F32 if out_f32 else BF16
    xt_v = xt.rearrange("(k p) t -> p k t", p=128)
    wt_v = wt.rearrange("(k p) o -> p k o", p=128)
    KG = 8
    n_xdma = (KC + KG - 1) // KG
    aux_dt = F32 if accum_in else BF16

    with ExitStack() as es:
        E = es.enter_context
        wblk = E(nc.sbuf_tensor(P + "wblk", [128, 2 * KC * OBW], BF16))
        xblk = E(nc.sbuf_tensor(P + "xblk", [128, KC * T], BF16))
        evac = E(nc.sbuf_tensor(P + "evac", [128, 2 * 512], out_dt))
        aux = E(nc.sbuf_tensor(P + "aux", [128, 2 * 512], aux_dt))
        cols = E(nc.sbuf_tensor(P + "cols", [128, max(OC, 1)], F32))
        ps = E(nc.psum_tensor(P + "ps", [128, 4 * 512], F32))
        ldw0 = E(nc.semaphore(P + "ldw0"))
        ldw1 = E(nc.semaphore(P + "ldw1"))
        lcol = E(nc.semaphore(P + "lcol"))
        lax0 = E(nc.semaphore(P + "lax0"))
        lax1 = E(nc.semaphore(P + "lax1"))
        ldx = E(nc.semaphore(P + "ldx"))
        mm = E(nc.semaphore(P + "mm"))
        ev = E(nc.semaphore(P + "ev"))
        st0 = E(nc.semaphore(P + "st0"))
        st1 = E(nc.semaphore(P + "st1"))
        block = E(nc.Block())
        WPD = max(1, (1024 // OBW))
        n_wd = (KC + WPD - 1) // WPD
        w_done = [n_wd * 16 * (obi // 2 + 1) for obi in range(OB)]

        def wsl(obi, k):
            return wblk[:, (obi % 2) * KC * OBW + k * OBW:
                        (obi % 2) * KC * OBW + (k + 1) * OBW]

        def emit_store(se, g):
            oi, it = g % OT, g // OT
            ti = it % TS
            o = (g // (TS * OT)) * OT + oi
            se.wait_ge(ev, g + 1)
            se.dma_start(out[o * 128:(o + 1) * 128, ti * 512:(ti + 1) * 512],
                         evac[:, (g % 2) * 512:(g % 2) * 512 + 512]
                         ).then_inc(st0 if g % 2 == 0 else st1, 16)

        @block.gpsimd
        def _(gp):
            if colv is not None:
                gp.dma_start(cols[:, 0:OC], colv).then_inc(lcol, 16)
            for obi in range(OB):
                if obi >= 2:
                    gp.wait_ge(mm, (obi - 1) * TS * OT)
                for k0 in range(0, KC, WPD):
                    kn = min(WPD, KC - k0)
                    gp.dma_start(
                        wblk[:, (obi % 2) * KC * OBW + k0 * OBW:
                             (obi % 2) * KC * OBW + (k0 + kn) * OBW]
                        .rearrange("p (k c) -> p k c", c=OBW),
                        wt_v[:, k0:k0 + kn, obi * OBW:(obi + 1) * OBW],
                    ).then_inc(ldw0 if obi % 2 == 0 else ldw1, 16)
        @block.sync
        def _(sync):
            for k0 in range(0, KC, KG):
                kn = min(KG, KC - k0)
                sync.dma_start(
                    xblk[:, k0 * T:(k0 + kn) * T]
                    .rearrange("p (k c) -> p k c", c=T),
                    xt_v[:, k0:k0 + kn, :],
                ).then_inc(ldx, 16)
            if mult_in or accum_in:
                src = min_ if mult_in else acc
                for g in range(NT):
                    oi, it = g % OT, g // OT
                    ti = it % TS
                    o = (g // (TS * OT)) * OT + oi
                    if g >= 2:
                        sync.wait_ge(ev, g - 1)
                    sync.dma_start(
                        aux[:, (g % 2) * 512:(g % 2) * 512 + 512],
                        src[o * 128:(o + 1) * 128,
                            ti * 512:(ti + 1) * 512]).then_inc(
                                lax0 if g % 2 == 0 else lax1, 16)

        @block.tensor
        def _(t):
            t.wait_ge(ldx, n_xdma * 16)
            for obi in range(OB):
                t.wait_ge(ldw0 if obi % 2 == 0 else ldw1, w_done[obi])
                for ti in range(TS):
                    for oi in range(OT):
                        g = (obi * TS + ti) * OT + oi
                        if g >= 4:
                            t.wait_ge(ev, g - 3)
                        pst = ps[:, (g % 4) * 512:(g % 4) * 512 + 512]
                        last = None
                        for k in range(KC):
                            last = t.matmul(
                                pst,
                                wsl(obi, k)[:, oi * 128:(oi + 1) * 128],
                                xblk[:, k * T + ti * 512:k * T + (ti + 1) * 512],
                                start=(k == 0), stop=(k == KC - 1))
                        last.then_inc(mm, 1)

        if vec_evac:
            @block.vector
            def _(v):
                op0 = AluOpType.add if obias else (
                    AluOpType.mult if oscale else AluOpType.bypass)
                op1 = AluOpType.add if accum_in else AluOpType.mult
                for g in range(NT):
                    oi = g % OT
                    o = (g // (TS * OT)) * OT + oi
                    v.wait_ge(mm, g + 1)
                    v.wait_ge(lax0 if g % 2 == 0 else lax1, (g // 2 + 1) * 16)
                    if colv is not None:
                        v.wait_ge(lcol, 16)
                    if g >= 2:
                        v.wait_ge(st0 if g % 2 == 0 else st1, (g // 2) * 16)
                    pst = ps[:, (g % 4) * 512:(g % 4) * 512 + 512]
                    ev_sl = evac[:, (g % 2) * 512:(g % 2) * 512 + 512]
                    if colv is not None:
                        v.scalar_tensor_tensor(
                            ev_sl, pst, cols[:, o:o + 1],
                            aux[:, (g % 2) * 512:(g % 2) * 512 + 512],
                            op0=op0, op1=op1).then_inc(ev, 1)
                    else:
                        v.tensor_tensor(
                            ev_sl, pst,
                            aux[:, (g % 2) * 512:(g % 2) * 512 + 512],
                            op=op1).then_inc(ev, 1)

            @block.scalar
            def _(sc):
                for g in range(NT):
                    emit_store(sc, g)
                sc.wait_ge(st0, ((NT + 1) // 2) * 16)
                sc.wait_ge(st1, (NT // 2) * 16)
        else:
            @block.scalar
            def _(sc):
                fn = act if act is not None else AF.Copy
                for g in range(NT):
                    oi = g % OT
                    o = (g // (TS * OT)) * OT + oi
                    sc.wait_ge(mm, g + 1)
                    if oscale:
                        sc.wait_ge(lcol, 16)
                    if g >= 2:
                        sc.wait_ge(st0 if g % 2 == 0 else st1, (g // 2) * 16)
                    pst = ps[:, (g % 4) * 512:(g % 4) * 512 + 512]
                    ev_sl = evac[:, (g % 2) * 512:(g % 2) * 512 + 512]
                    kw = {}
                    if oscale:
                        kw["scale"] = cols[:, o:o + 1]
                    sc.activation(ev_sl, pst, fn, **kw).then_inc(ev, 1)
                    emit_store(sc, g)
                sc.wait_ge(st0, ((NT + 1) // 2) * 16)
                sc.wait_ge(st1, (NT // 2) * 16)


def emit_vtok(nc, P, xt, wt, vp, T=2048, DH=64):
    KT = T // 128
    H = 1024 // DH
    DHp = vp.shape[3]
    hps = 512 // DH
    NW = H * KT * DHp
    with ExitStack() as es:
        E = es.enter_context
        xnts = E(nc.sbuf_tensor(P + "xnts", [128, 8 * T], BF16))
        wts = E(nc.sbuf_tensor(P + "wts", [128, 8 * 1024], BF16))
        stg = E(nc.sbuf_tensor(P + "stg", [128, 2 * 512], BF16))
        ones = (E(nc.sbuf_tensor(P + "ones", [128, NW], BF16))
                if DHp != DH else None)
        psv = E(nc.psum_tensor(P + "psv", [128, 2 * 512], F32))
        ld = E(nc.semaphore(P + "ld"))
        ldi = E(nc.semaphore(P + "ldi"))
        ldj = E(nc.semaphore(P + "ldj"))
        mm = E(nc.semaphore(P + "mm"))
        ev = E(nc.semaphore(P + "ev"))
        st0 = E(nc.semaphore(P + "st0"))
        st1 = E(nc.semaphore(P + "st1"))
        block = E(nc.Block())

        @block.gpsimd
        def _(g):
            if ones is not None:
                g.memset(ones[:, :], 1.0).then_inc(ldi, 1)
                g.wait_ge(ldi, 1)
                g.dma_start(vp.rearrange("p h k c -> p (h k c)"),
                            ones[:, :]).then_inc(ldj, 16)
            g.dma_start(xnts[:, :].rearrange("p (c t) -> p c t", t=T),
                        xt.rearrange("(c p) t -> p c t", p=128)).then_inc(ld, 16)
            g.dma_start(wts[:, :].rearrange("p (c o) -> p c o", o=1024),
                        wt.rearrange("(c p) o -> p c o", p=128)).then_inc(ld, 16)
            if ones is not None:
                g.wait_ge(ldj, 16)
            for g_i in range(2 * KT):
                kt_, j = g_i // 2, g_i % 2
                g.wait_ge(ev, g_i + 1)
                g.dma_start(vp[:, j * hps:(j + 1) * hps, kt_, 0:DH],
                            stg[:, (g_i % 2) * 512:(g_i % 2) * 512 + 512]
                            .rearrange("p (h c) -> p h c", c=DH)
                            ).then_inc(st0 if g_i % 2 == 0 else st1, 16)
            g.wait_ge(st0, KT * 16)
            g.wait_ge(st1, KT * 16)

        @block.tensor
        def _(t):
            t.wait_ge(ld, 32)
            for g_i in range(2 * KT):
                kt_, j = g_i // 2, g_i % 2
                if g_i >= 2:
                    t.wait_ge(ev, g_i - 1)
                last = None
                for c in range(8):
                    last = t.matmul(
                        psv[:, (g_i % 2) * 512:(g_i % 2) * 512 + 512],
                        xnts[:, c * T + kt_ * 128:c * T + (kt_ + 1) * 128],
                        wts[:, c * 1024 + j * 512:c * 1024 + (j + 1) * 512],
                        start=(c == 0), stop=(c == 7))
                last.then_inc(mm, 1)

        @block.vector
        def _(v):
            for g_i in range(2 * KT):
                v.wait_ge(mm, g_i + 1)
                if g_i >= 2:
                    v.wait_ge(st0 if g_i % 2 == 0 else st1, (g_i // 2) * 16)
                v.tensor_copy(stg[:, (g_i % 2) * 512:(g_i % 2) * 512 + 512],
                              psv[:, (g_i % 2) * 512:(g_i % 2) * 512 + 512]
                              ).then_inc(ev, 1)


def emit_attn0(nc, P, xnt, wqk, vp, masks, zt, T=2048, H=16):
    KT = T // 128
    QS = T // 512
    HP = H // 2
    NQT = 2 * QS
    wqk_v = wqk.rearrange("(k p) o -> p k o", p=128)
    with ExitStack() as es:
        E = es.enter_context
        xns = E(nc.sbuf_tensor(P + "xns", [128, 8 * T], BF16))
        qb = E(nc.sbuf_tensor(P + "qb", [128, 2 * T], BF16))
        kb = E(nc.sbuf_tensor(P + "kb", [128, 2 * T], BF16))
        wqs = E(nc.sbuf_tensor(P + "wqs", [128, 2 * 2048], BF16))
        vps = E(nc.sbuf_tensor(P + "vps", [128, H * KT * 65], BF16))
        msk = E(nc.sbuf_tensor(P + "msk", [128, 4 * 512], BF16))
        expt = E(nc.sbuf_tensor(P + "expt", [128, 2 * 2 * KT * 512], BF16))
        srec = E(nc.sbuf_tensor(P + "srec", [128, 2 * 512], F32))
        recB = E(nc.sbuf_tensor(P + "recB", [128, 2 * 512], F32))
        zst = E(nc.sbuf_tensor(P + "zst", [128, 2 * 512], BF16))
        onesb = E(nc.sbuf_tensor(P + "onesb", [128, 64], F32))
        ps_s = E(nc.psum_tensor(P + "ps_s", [128, 4 * 512], F32))
        ps_z0 = E(nc.psum_tensor(P + "ps_z0", [128, 512], F32))
        ps_z1 = E(nc.psum_tensor(P + "ps_z1", [128, 512], F32))
        ps_b = E(nc.psum_tensor(P + "ps_b", [128, 512], F32))
        ps_q = E(nc.psum_tensor(P + "ps_q", [128, 512], F32))
        ini = E(nc.semaphore(P + "ini"))
        ld = E(nc.semaphore(P + "ld"))
        ldq0 = E(nc.semaphore(P + "ldq0"))
        ldq1 = E(nc.semaphore(P + "ldq1"))
        s_qk = E(nc.semaphore(P + "s_qk"))
        s_qe = E(nc.semaphore(P + "s_qe"))
        s_mm = E(nc.semaphore(P + "s_mm"))
        s_exp = E(nc.semaphore(P + "s_exp"))
        s_msk = E(nc.semaphore(P + "s_msk"))
        s_av = E(nc.semaphore(P + "s_av"))
        s_rc = E(nc.semaphore(P + "s_rc"))
        s_bc = E(nc.semaphore(P + "s_bc"))
        s_ev = E(nc.semaphore(P + "s_ev"))
        st0 = E(nc.semaphore(P + "st0"))
        st1 = E(nc.semaphore(P + "st1"))
        block = E(nc.Block())

        groups = [(hp, qs) for hp in range(HP) for qs in range(QS)]
        NG0 = len(groups)
        exp_base, msk_base, mm_base = [], [], []
        e = m = s = 0
        for (hp, qs) in groups:
            exp_base.append(e); msk_base.append(m); mm_base.append(s)
            e += 4 * qs + 4
            m += 8
            s += 2 * (4 * qs + 4)
        mm_total = s

        def mmb(gidx):
            return mm_base[gidx] if gidx < NG0 else mm_total

        @block.gpsimd
        def _(g):
            g.memset(onesb[:, :], 1.0).then_inc(ini, 1)
            g.dma_start(xns[:, :].rearrange("p (c t) -> p c t", t=T),
                        xnt.rearrange("(c p) t -> p c t", p=128)).then_inc(ld, 16)
            g.dma_start(vps[:, :].rearrange("p (h k c) -> p h k c", k=KT, c=65),
                        vp[:, :, :, :]).then_inc(ld, 16)
            g.dma_start(msk[:, :].rearrange("p (j c) -> p j c", c=512),
                        masks[:, :, :]).then_inc(ld, 16)
            def wload(hp):
                par = hp % 2
                if hp >= 2:
                    g.wait_ge(s_qk, NQT * (hp - 1))
                g.dma_start(wqs[:, par * 2048:par * 2048 + 1024]
                            .rearrange("p (k c) -> p k c", c=128),
                            wqk_v[:, :, 128 * hp:128 * hp + 128]).then_inc(
                                ldq0 if par == 0 else ldq1, 16)
                g.dma_start(wqs[:, par * 2048 + 1024:par * 2048 + 2048]
                            .rearrange("p (k c) -> p k c", c=128),
                            wqk_v[:, :, 1024 + 128 * hp:1024 + 128 * hp + 128]
                            ).then_inc(ldq0 if par == 0 else ldq1, 16)

            wload(0)
            if HP > 1:
                wload(1)
            for g_i, (hp, qs) in enumerate(groups):
                if qs == 0 and hp + 2 < HP:
                    wload(hp + 2)
                ep = (g_i % 2) * 2 * KT * 512
                for j in range(4):
                    k = 4 * qs + j
                    g.wait_ge(s_exp, exp_base[g_i] + k + 1)
                    for hh in range(2):
                        g.tensor_tensor(expt[:, ep + (2 * k + hh) * 512:
                                             ep + (2 * k + hh + 1) * 512],
                                        expt[:, ep + (2 * k + hh) * 512:
                                             ep + (2 * k + hh + 1) * 512],
                                        msk[:, j * 512:(j + 1) * 512],
                                        op=AluOpType.mult).then_inc(s_msk, 1)
                for hh in range(2):
                    g.wait_ge(s_ev, 4 * g_i + 2 * hh + 2)
                    g.dma_start(zt[64 * (2 * hp + hh):64 * (2 * hp + hh) + 64,
                                   qs * 512:(qs + 1) * 512],
                                zst[0:64, hh * 512:hh * 512 + 512]
                                ).then_inc(st0 if hh == 0 else st1, 16)
            g.wait_ge(st0, NG0 * 16)
            g.wait_ge(st1, NG0 * 16)

        @block.vector
        def _(v):
            v.wait_ge(ld, 48)
            v.memset(vps[:, :].rearrange("p (g c) -> p g c", c=65)[:, :, 64:65],
                     1.0).then_inc(ini, 1)

            def qk_evacs(hp):
                par = hp % 2
                if hp >= 2:
                    v.wait_ge(s_mm, mmb(QS * (hp - 1)))
                for j in range(NQT):
                    ii = hp * NQT + j
                    v.wait_ge(s_qk, ii + 1)
                    dst = qb if j < QS else kb
                    tsl = j % QS
                    v.tensor_copy(dst[:, par * T + tsl * 512:par * T + (tsl + 1) * 512],
                                  ps_q[:, :]).then_inc(s_qe, 1)

            qk_evacs(0)
            for g_i, (hp, qs) in enumerate(groups):
                if qs == 0 and hp + 1 < HP:
                    qk_evacs(hp + 1)
                for hh in range(2):
                    psz = ps_z0 if hh == 0 else ps_z1
                    v.wait_ge(s_av, 2 * g_i + hh + 1)
                    if g_i > 0 or hh > 0:
                        v.wait_ge(s_bc, 2 * g_i + hh)
                    v.reciprocal(srec[64:65, hh * 512:hh * 512 + 512],
                                 psz[64:65, :]).then_inc(s_rc, 1)
                    v.wait_ge(s_bc, 2 * g_i + hh + 1)
                    v.tensor_copy(recB[0:64, hh * 512:hh * 512 + 512],
                                  ps_b[0:64, :]).then_inc(s_ev, 1)
                    if g_i >= 1:
                        v.wait_ge(st0 if hh == 0 else st1, g_i * 16)
                    v.wait_ge(s_ev, 4 * g_i + 2 * hh + 1)
                    v.tensor_tensor(zst[0:64, hh * 512:hh * 512 + 512],
                                    psz[0:64, :],
                                    recB[0:64, hh * 512:hh * 512 + 512],
                                    op=AluOpType.mult).then_inc(s_ev, 1)

        @block.tensor
        def _(t):
            t.wait_ge(ld, 48)
            t.wait_ge(ini, 2)

            def qk_mms(hp):
                par = hp % 2
                out = []
                for j in range(NQT):
                    def f(hp=hp, j=j, par=par):
                        ii = hp * NQT + j
                        t.wait_ge(ldq0 if par == 0 else ldq1,
                                  (hp // 2 + 1) * 32)
                        if ii > 0:
                            t.wait_ge(s_qe, ii)
                        tsl = j % QS
                        wof = par * 2048 + (0 if j < QS else 1024)
                        last = None
                        for c in range(8):
                            last = t.matmul(
                                ps_q[:, :],
                                wqs[:, wof + c * 128:wof + (c + 1) * 128],
                                xns[:, c * T + tsl * 512:c * T + (tsl + 1) * 512],
                                start=(c == 0), stop=(c == 7))
                        last.then_inc(s_qk, 1)
                    out.append(f)
                return out

            for f in qk_mms(0):
                f()

            def a_work(g):
                hp, qs = groups[g]
                ktmax = 4 * qs + 4
                ep = (g % 2) * 2 * KT * 512
                work = []
                for hh in range(2):
                    h = 2 * hp + hh
                    psz = ps_z0 if hh == 0 else ps_z1

                    def pre(g=g, hh=hh):
                        if g > 0:
                            t.wait_ge(s_ev, 4 * (g - 1) + 2 * hh + 2)
                    work.append(pre)
                    for k in range(ktmax):
                        def f(g=g, hh=hh, k=k, psz=psz, h=h, qs=qs,
                              ktmax=ktmax, ep=ep):
                            t.wait_ge(s_exp, exp_base[g] + k + 1)
                            if k >= 4 * qs:
                                t.wait_ge(s_msk, msk_base[g] +
                                          2 * (k - 4 * qs) + hh + 1)
                            t.matmul(psz[0:65, :],
                                     vps[:, (h * KT + k) * 65:(h * KT + k + 1) * 65],
                                     expt[:, ep + (2 * k + hh) * 512:
                                          ep + (2 * k + hh + 1) * 512],
                                     start=(k == 0), stop=(k == ktmax - 1)
                                     ).then_inc(s_av, 1 if k == ktmax - 1 else 0)
                        work.append(f)

                    def bc(g=g, hh=hh):
                        t.wait_ge(s_rc, 2 * g + hh + 1)
                        w = 4 * g + 3 * hh - 1
                        if w > 0:
                            t.wait_ge(s_ev, w)
                        t.matmul(ps_b[0:64, :],
                                 onesb[64:65, 0:64],
                                 srec[64:65, hh * 512:hh * 512 + 512],
                                 start=True, stop=True).then_inc(s_bc, 1)
                    work.append(bc)
                return work

            pend = []
            qkq = []
            for g_i in range(NG0 + 1):
                if g_i < NG0:
                    hp, qs = groups[g_i]
                    if qs == 0 and hp + 1 < HP:
                        qkq = qk_mms(hp + 1)
                    ktmax = 4 * qs + 4
                    hc = (hp % 2) * T
                    nper = max(1, (len(pend) + ktmax - 1) // ktmax)
                    nqk = max(1, (len(qkq) + 4 * ktmax - 1) // (4 * ktmax)) if qkq else 0
                    t.wait_ge(s_qe, NQT * (hp + 1))
                    for k in range(ktmax):
                        if exp_base[g_i] + k >= 2:
                            t.wait_ge(s_exp, exp_base[g_i] + k - 1)
                        for hh in range(2):
                            r0 = 64 * hh
                            t.matmul(ps_s[:, (2 * (k % 2) + hh) * 512:
                                          (2 * (k % 2) + hh) * 512 + 512],
                                     kb[r0:r0 + 64, hc + k * 128:hc + (k + 1) * 128],
                                     qb[r0:r0 + 64, hc + qs * 512:hc + (qs + 1) * 512],
                                     start=True, stop=True,
                                     tile_position=(r0, 0)).then_inc(s_mm, 1)
                        for _ in range(nper):
                            if pend:
                                pend.pop(0)()
                    for f in qkq:
                        f()
                    qkq = []
                for f in pend:
                    f()
                pend = a_work(g_i) if g_i < NG0 else []

        @block.scalar
        def _(sc):
            sc.wait_ge(ini, 2)
            for g_i, (hp, qs) in enumerate(groups):
                ktmax = 4 * qs + 4
                ep = (g_i % 2) * 2 * KT * 512
                for k in range(ktmax):
                    sc.wait_ge(s_mm, mm_base[g_i] + 2 * k + 2)
                    if k == 0 and g_i >= 2:
                        sc.wait_ge(s_av, 2 * (g_i - 1))
                    sc.activation(expt[:, ep + 2 * k * 512:ep + (2 * k + 2) * 512],
                                  ps_s[:, 2 * (k % 2) * 512:(2 * (k % 2) + 2) * 512],
                                  AF.Exp, scale=0.125).then_inc(s_exp, 1)


def emit_attn1(nc, P, qt, kt, vh, dw, zt, T=2048, H=8):
    KT = T // 128
    QS = T // 512
    with ExitStack() as es:
        E = es.enter_context
        qts = E(nc.sbuf_tensor(P + "qts", [128, 8 * T], BF16))
        kts = E(nc.sbuf_tensor(P + "kts", [128, 8 * T], BF16))
        vhs = E(nc.sbuf_tensor(P + "vhs", [128, H * KT * 128], BF16))
        expt = E(nc.sbuf_tensor(P + "expt", [128, 2 * 2 * KT * 512], BF16))
        srec = E(nc.sbuf_tensor(P + "srec", [1, 2 * 512], F32))
        recB = E(nc.sbuf_tensor(P + "recB", [128, 2 * 512], F32))
        tmp0 = E(nc.sbuf_tensor(P + "tmp0", [128, 512], F32))
        tmp1 = E(nc.sbuf_tensor(P + "tmp1", [128, 512], F32))
        zst = E(nc.sbuf_tensor(P + "zst", [128, 2 * 512], BF16))
        onesc = E(nc.sbuf_tensor(P + "onesc", [128, 1], BF16))
        onesr = E(nc.sbuf_tensor(P + "onesr", [1, 128], F32))
        ndr = E(nc.sbuf_tensor(P + "ndr", [1, 128], F32))
        dwsb = E(nc.sbuf_tensor(P + "dwsb", [1, 1], F32))
        ps_s = E(nc.psum_tensor(P + "ps_s", [128, 4 * 512], F32))
        ps_sum = E(nc.psum_tensor(P + "ps_sum", [128, 2 * 512], F32))
        ps_z0 = E(nc.psum_tensor(P + "ps_z0", [128, 512], F32))
        ps_z1 = E(nc.psum_tensor(P + "ps_z1", [128, 512], F32))
        ini = E(nc.semaphore(P + "ini"))
        ld = E(nc.semaphore(P + "ld"))
        s_mm = E(nc.semaphore(P + "s_mm"))
        s_exp = E(nc.semaphore(P + "s_exp"))
        s_av = E(nc.semaphore(P + "s_av"))
        s_rc = E(nc.semaphore(P + "s_rc"))
        s_bc = E(nc.semaphore(P + "s_bc"))
        s_ev = E(nc.semaphore(P + "s_ev"))
        s_nd = E(nc.semaphore(P + "s_nd"))
        st0 = E(nc.semaphore(P + "st0"))
        st1 = E(nc.semaphore(P + "st1"))
        block = E(nc.Block())
        groups = [(h, qs) for h in range(H) for qs in range(QS)]
        NG = len(groups)

        @block.gpsimd
        def _(g):
            g.memset(onesc[:, :], 1.0).then_inc(ini, 1)
            g.memset(onesr[:, :], 1.0).then_inc(ini, 1)
            g.dma_start(qts[:, :].rearrange("p (c t) -> p c t", t=T),
                        qt.rearrange("(c p) t -> p c t", p=128)).then_inc(ld, 16)
            g.dma_start(kts[:, :].rearrange("p (c t) -> p c t", t=T),
                        kt.rearrange("(c p) t -> p c t", p=128)).then_inc(ld, 16)
            g.dma_start(vhs[:, :].rearrange("p (h k c) -> p h k c", k=KT, c=128),
                        vh[:, :, :, :]).then_inc(ld, 16)
            g.dma_start(dwsb[:, :], dw).then_inc(ld, 16)
            for g_i, (h, qs) in enumerate(groups):
                g.wait_ge(s_ev, 5 * g_i + 5)
                g.dma_start(zt[128 * h:128 * h + 128, qs * 512:(qs + 1) * 512],
                            zst[:, (g_i % 2) * 512:(g_i % 2) * 512 + 512]
                            ).then_inc(st0 if g_i % 2 == 0 else st1, 16)
            g.wait_ge(st0, ((NG + 1) // 2) * 16)
            g.wait_ge(st1, (NG // 2) * 16)

        @block.tensor
        def _(t):
            t.wait_ge(ld, 64)
            t.wait_ge(ini, 2)
            t.matmul(ps_z1[0:1, 0:128], dwsb[:, :], onesr[:, :],
                     start=True, stop=True).then_inc(s_nd, 1)

            def a_work(g):
                h, qs = groups[g]
                ep = (g % 2) * 2 * KT * 512
                eb = KT * g
                work = []

                def pre():
                    if g > 0:
                        t.wait_ge(s_ev, 5 * g - 1)
                        t.wait_ge(s_rc, 2 * g)
                    if g == 0:
                        t.wait_ge(s_nd, 2)
                work.append(pre)
                for i in range(2):
                    psz = ps_z0 if i == 0 else ps_z1
                    for k in range(KT):
                        def f(i=i, k=k, psz=psz):
                            t.wait_ge(s_exp, eb + k + 1)
                            t.matmul(ps_sum[0:1, i * 512:i * 512 + 512],
                                     onesc[:, :],
                                     expt[:, ep + (2 * k + i) * 512:
                                          ep + (2 * k + i + 1) * 512],
                                     start=(k == 0), stop=(k == KT - 1))
                            t.matmul(psz[:, :],
                                     vhs[:, (h * KT + k) * 128:(h * KT + k + 1) * 128],
                                     expt[:, ep + (2 * k + i) * 512:
                                          ep + (2 * k + i + 1) * 512],
                                     start=(k == 0), stop=(k == KT - 1)
                                     ).then_inc(s_av, 1 if k == KT - 1 else 0)
                        work.append(f)

                def bc():
                    t.wait_ge(s_rc, 2 * g + 2)
                    t.matmul(ps_sum[:, 0:512], onesr[:, :], srec[:, 0:512],
                             start=True, stop=True).then_inc(s_bc, 1)
                    t.matmul(ps_sum[:, 512:1024], ndr[:, :], srec[:, 512:1024],
                             start=True, stop=True).then_inc(s_bc, 1)
                work.append(bc)
                return work

            pend = []
            for g_i in range(NG + 1):
                if g_i < NG:
                    h, qs = groups[g_i]
                    ep = (g_i % 2) * 2 * KT * 512
                    eb = KT * g_i
                    nper = max(1, (len(pend) + KT - 1) // KT)
                    for k in range(KT):
                        if g_i + k > 0:
                            t.wait_ge(s_exp, eb + k - 1)
                        for i in range(2):
                            r0 = i * 64
                            t.matmul(ps_s[:, (2 * (k % 2) + i) * 512:
                                          (2 * (k % 2) + i) * 512 + 512],
                                     kts[r0:r0 + 64, h * T + k * 128:
                                         h * T + (k + 1) * 128],
                                     qts[r0:r0 + 64, h * T + qs * 512:
                                         h * T + (qs + 1) * 512],
                                     start=True, stop=True,
                                     tile_position=(r0, 0)).then_inc(s_mm, 1)
                        for _ in range(nper):
                            if pend:
                                pend.pop(0)()
                for f in pend:
                    f()
                pend = a_work(g_i) if g_i < NG else []

        @block.scalar
        def _(sc):
            sc.wait_ge(s_nd, 1)
            sc.activation(ndr[:, :], ps_z1[0:1, 0:128], AF.Copy,
                          scale=-1.0).then_inc(s_nd, 1)
            for g_i, (h, qs) in enumerate(groups):
                ebm = 2 * KT * g_i
                ep = (g_i % 2) * 2 * KT * 512
                for k in range(KT):
                    sc.wait_ge(s_mm, ebm + 2 * k + 2)
                    if k == 0 and g_i >= 2:
                        sc.wait_ge(s_av, 2 * (g_i - 1))
                    sc.activation(expt[:, ep + 2 * k * 512:ep + (2 * k + 2) * 512],
                                  ps_s[:, 2 * (k % 2) * 512:(2 * (k % 2) + 2) * 512],
                                  AF.Exp, scale=0.125).then_inc(s_exp, 1)

        @block.vector
        def _(v):
            for g_i, (h, qs) in enumerate(groups):
                v.wait_ge(s_av, 2 * g_i + 1)
                if g_i > 0:
                    v.wait_ge(s_bc, 2 * g_i)
                v.reciprocal(srec[:, 0:512], ps_sum[0:1, 0:512]).then_inc(s_rc, 1)
                v.wait_ge(s_av, 2 * g_i + 2)
                v.reciprocal(srec[:, 512:1024], ps_sum[0:1, 512:1024]).then_inc(s_rc, 1)
                v.wait_ge(s_bc, 2 * g_i + 1)
                v.tensor_copy(recB[:, 0:512], ps_sum[:, 0:512]).then_inc(s_ev, 1)
                v.wait_ge(s_bc, 2 * g_i + 2)
                v.tensor_copy(recB[:, 512:1024], ps_sum[:, 512:1024]).then_inc(s_ev, 1)
                v.wait_ge(s_ev, 5 * g_i + 1)
                v.tensor_tensor(tmp0[:, :], ps_z0[:, :], recB[:, 0:512],
                                op=AluOpType.mult).then_inc(s_ev, 1)
                v.wait_ge(s_ev, 5 * g_i + 2)
                v.tensor_tensor(tmp1[:, :], ps_z1[:, :], recB[:, 512:1024],
                                op=AluOpType.mult).then_inc(s_ev, 1)
                v.wait_ge(s_ev, 5 * g_i + 4)
                if g_i >= 2:
                    v.wait_ge(st0 if g_i % 2 == 0 else st1, (g_i // 2) * 16)
                v.tensor_tensor(zst[:, (g_i % 2) * 512:(g_i % 2) * 512 + 512],
                                tmp0[:, :], tmp1[:, :],
                                op=AluOpType.add).then_inc(s_ev, 1)


def emit_rms(nc, P, ztin, nw, bo, dwin, nt, bec_out, T=2048, D=1024):
    C, NS = D // 128, T // 512
    EPS = float(np.finfo(np.float32).eps)
    with ExitStack() as es:
        E = es.enter_context
        zts = E(nc.sbuf_tensor(P + "zts", [128, C * T], BF16))
        sqb = E(nc.sbuf_tensor(P + "sqb", [128, 2 * T], BF16))
        nwc = E(nc.sbuf_tensor(P + "nwc", [128, C], F32))
        boc = E(nc.sbuf_tensor(P + "boc", [128, C], F32))
        bec = E(nc.sbuf_tensor(P + "bec", [128, C], F32))
        dwsb = E(nc.sbuf_tensor(P + "dwsb", [1, 1], F32))
        odw = E(nc.sbuf_tensor(P + "odw", [1, 1], F32))
        odwB = E(nc.sbuf_tensor(P + "odwB", [128, 1], F32))
        epsc = E(nc.sbuf_tensor(P + "epsc", [1, 1], F32))
        thr3 = E(nc.sbuf_tensor(P + "thr3", [1, 1], F32))
        er3 = E(nc.sbuf_tensor(P + "er3", [1, 1], F32))
        onesc = E(nc.sbuf_tensor(P + "onesc", [128, 1], BF16))
        onesr = E(nc.sbuf_tensor(P + "onesr", [1, 128], F32))
        rows = E(nc.sbuf_tensor(P + "rows", [1, 2 * T], F32))
        rstdB = E(nc.sbuf_tensor(P + "rstdB", [128, T], F32))
        xout = E(nc.sbuf_tensor(P + "xout", [128, 2 * T], BF16))
        psA = E(nc.psum_tensor(P + "psA", [128, T], F32))
        psB = E(nc.psum_tensor(P + "psB", [128, T], F32))
        ini = E(nc.semaphore(P + "ini"))
        epse = E(nc.semaphore(P + "epse"))
        ld = E(nc.semaphore(P + "ld"))
        casts = E(nc.semaphore(P + "casts"))
        mm = E(nc.semaphore(P + "mm"))
        rw = E(nc.semaphore(P + "rw"))
        ev = E(nc.semaphore(P + "ev"))
        st0 = E(nc.semaphore(P + "st0"))
        st1 = E(nc.semaphore(P + "st1"))
        block = E(nc.Block())
        row_s, row_r = rows[:, 0:T], rows[:, T:2 * T]

        @block.gpsimd
        def _(g):
            g.memset(onesc[:, :], 1.0).then_inc(ini, 1)
            g.memset(onesr[:, :], 1.0).then_inc(ini, 1)
            g.memset(thr3[:, :], 3.0).then_inc(ini, 1)
            g.dma_start(zts[:, :].rearrange("p (c t) -> p c t", t=T),
                        ztin.rearrange("(c p) t -> p c t", p=128)).then_inc(ld, 16)
            g.dma_start(nwc[:, :], nw).then_inc(ld, 16)
            g.dma_start(boc[:, :], bo).then_inc(ld, 16)
            g.dma_start(dwsb[:, :], dwin).then_inc(ld, 16)
            for c in range(C):
                g.wait_ge(ev, NS + 2 + c + 1)
                g.dma_start(nt.rearrange("(c p) t -> p c t", p=128)[:, c, :],
                            xout[:, (c % 2) * T:(c % 2 + 1) * T]
                            ).then_inc(st0 if c % 2 == 0 else st1, 16)
            g.wait_ge(ev, NS + 2)
            g.dma_start(bec_out, bec[:, :]).then_inc(st0, 16)
            g.wait_ge(st0, (C // 2 + 1) * 16)
            g.wait_ge(st1, (C // 2) * 16)

        @block.scalar
        def _(sc):
            sc.wait_ge(ld, 64)
            sc.wait_ge(ini, 3)
            for c in range(C):
                if c >= 2:
                    sc.wait_ge(mm, c - 1)
                sc.activation(sqb[:, (c % 2) * T:(c % 2 + 1) * T],
                              zts[:, c * T:(c + 1) * T], AF.Square).then_inc(casts, 1)
            sc.wait_ge(epse, 3)
            sc.activation(epsc[:, :], epsc[:, :], AF.Abs).then_inc(epse, 1)
            sc.wait_ge(mm, C)
            sc.wait_ge(epse, 4)
            sc.activation(row_s, psA[0:1, :], AF.Sqrt, scale=1.0 / D,
                          bias=epsc[:, :]).then_inc(rw, 1)
            sc.activation(odw[:, :], dwsb[:, :], AF.Copy, scale=-1.0,
                          bias=1.0).then_inc(rw, 1)

        @block.vector
        def _(v):
            v.wait_ge(ini, 3)
            v.reciprocal(er3[:, :], thr3[:, :]).then_inc(epse, 1)
            v.wait_ge(epse, 1)
            v.tensor_scalar(epsc[:, :], er3[:, :], 4.0, -1.0,
                            op0=AluOpType.mult, op1=AluOpType.add).then_inc(epse, 1)
            v.wait_ge(epse, 2)
            v.tensor_scalar(epsc[:, :], epsc[:, :], 3.0, -1.0,
                            op0=AluOpType.mult, op1=AluOpType.add).then_inc(epse, 1)
            v.wait_ge(rw, 1)
            v.reciprocal(row_r, row_s).then_inc(rw, 1)
            v.wait_ge(rw, 3)
            v.tensor_scalar_mul(row_r, row_r, odw[:, :]).then_inc(rw, 1)
            for j in range(NS):
                v.wait_ge(mm, C + j + 1)
                v.tensor_copy(rstdB[:, j * 512:(j + 1) * 512],
                              psB[:, j * 512:(j + 1) * 512]).then_inc(ev, 1)
            v.wait_ge(mm, C + NS + 1)
            v.tensor_copy(odwB[:, :], psA[:, 0:1]).then_inc(ev, 1)
            v.wait_ge(ev, NS + 1)
            v.tensor_scalar_mul(bec[:, :], boc[:, :], odwB[:, :]).then_inc(ev, 1)
            for c in range(C):
                v.wait_ge(ev, NS + 2 + c)
                if c >= 2:
                    v.wait_ge(st0 if c % 2 == 0 else st1, (c // 2) * 16)
                v.scalar_tensor_tensor(xout[:, (c % 2) * T:(c % 2 + 1) * T],
                                       zts[:, c * T:(c + 1) * T], nwc[:, c:c + 1],
                                       rstdB[:, :], op0=AluOpType.mult,
                                       op1=AluOpType.mult).then_inc(ev, 1)

        @block.tensor
        def _(t):
            t.wait_ge(ini, 1)
            for c in range(C):
                t.wait_ge(casts, c + 1)
                last = None
                for j in range(NS):
                    last = t.matmul(psA[0:1, j * 512:(j + 1) * 512], onesc[:, :],
                                    sqb[:, (c % 2) * T + j * 512:(c % 2) * T + (j + 1) * 512],
                                    start=(c == 0), stop=(c == C - 1))
                last.then_inc(mm, 1)
            t.wait_ge(rw, 4)
            for j in range(NS):
                t.matmul(psB[:, j * 512:(j + 1) * 512], onesr[:, :],
                         row_r[:, j * 512:(j + 1) * 512], start=True,
                         stop=True).then_inc(mm, 1)
            t.wait_ge(rw, 4)
            t.matmul(psA[:, 0:1], onesr[:, :], odw[:, :], start=True,
                     stop=True).then_inc(mm, 1)


def emit_rails(nc, P, xnt, rmc, ruc, gz, T=2048):
    gz_v = gz.rearrange("(r p) t -> p r t", p=128)
    NS = T // 512
    with ExitStack() as es:
        E = es.enter_context
        xnts = E(nc.sbuf_tensor(P + "xnts", [128, 8 * T], BF16))
        rmcs = E(nc.sbuf_tensor(P + "rmcs", [128, 32 * 128], BF16))
        rucs = E(nc.sbuf_tensor(P + "rucs", [128, 128 * 128], BF16))
        ybuf = E(nc.sbuf_tensor(P + "ybuf", [128, 2 * T], BF16))
        zbuf = E(nc.sbuf_tensor(P + "zbuf", [128, 2 * T], BF16))
        ps_y = E(nc.psum_tensor(P + "ps_y", [128, 2 * 512], F32))
        ps_z = E(nc.psum_tensor(P + "ps_z", [128, 2 * 512], F32))
        ld = E(nc.semaphore(P + "ld"))
        s_ym = E(nc.semaphore(P + "s_ym"))
        s_ye = E(nc.semaphore(P + "s_ye"))
        s_zm = E(nc.semaphore(P + "s_zm"))
        s_ze = E(nc.semaphore(P + "s_ze"))
        st0 = E(nc.semaphore(P + "st0"))
        st1 = E(nc.semaphore(P + "st1"))
        block = E(nc.Block())

        @block.gpsimd
        def _(g):
            g.dma_start(xnts[:, :].rearrange("p (c t) -> p c t", t=T),
                        xnt.rearrange("(c p) t -> p c t", p=128)).then_inc(ld, 16)
            for i in range(4):
                g.dma_start(rmcs[:, i * 8 * 128:(i + 1) * 8 * 128]
                            .rearrange("p (g c) -> p g c", c=128),
                            rmc[:, i * 8:(i + 1) * 8, :]).then_inc(ld, 16)
            for i in range(16):
                g.dma_start(rucs[:, i * 8 * 128:(i + 1) * 8 * 128]
                            .rearrange("p (g c) -> p g c", c=128),
                            ruc[:, i * 8:(i + 1) * 8, :]).then_inc(ld, 16)
            for zg in range(128):
                g.wait_ge(s_ze, NS * zg + NS)
                g.dma_start(gz_v[:, zg, :],
                            zbuf[:, (zg % 2) * T:(zg % 2 + 1) * T]
                            ).then_inc(st0 if zg % 2 == 0 else st1, 16)
            g.wait_ge(st0, 64 * 16)
            g.wait_ge(st1, 64 * 16)

        @block.tensor
        def _(t):
            t.wait_ge(ld, 21 * 16)
            for gi in range(32):
                ro = gi % 8
                for s in range(NS):
                    ys = gi * NS + s
                    if ys >= 2:
                        t.wait_ge(s_ye, ys - 1)
                    t.matmul(ps_y[:, (s % 2) * 512:(s % 2) * 512 + 512],
                             rmcs[:, gi * 128:(gi + 1) * 128],
                             xnts[:, ro * T + s * 512:ro * T + (s + 1) * 512],
                             start=True, stop=True).then_inc(s_ym, 1)
                for zc in range(4):
                    for s in range(NS):
                        zi = (gi * 4 + zc) * NS + s
                        t.wait_ge(s_ye, gi * NS + s + 1)
                        if zi >= 2:
                            t.wait_ge(s_ze, zi - 1)
                        t.matmul(ps_z[:, (zi % 2) * 512:(zi % 2) * 512 + 512],
                                 rucs[:, (gi * 4 + zc) * 128:(gi * 4 + zc + 1) * 128],
                                 ybuf[:, (gi % 2) * T + s * 512:(gi % 2) * T + (s + 1) * 512],
                                 start=True, stop=True).then_inc(s_zm, 1)

        @block.scalar
        def _(sc):
            for gi in range(32):
                for s in range(NS):
                    ys = gi * NS + s
                    sc.wait_ge(s_ym, ys + 1)
                    if gi >= 2:
                        sc.wait_ge(s_zm, 4 * NS * (gi - 1))
                    sc.activation(ybuf[:, (gi % 2) * T + s * 512:(gi % 2) * T + (s + 1) * 512],
                                  ps_y[:, (s % 2) * 512:(s % 2) * 512 + 512],
                                  AF.Copy).then_inc(s_ye, 1)
                for zc in range(4):
                    zg = gi * 4 + zc
                    for s in range(NS):
                        zi = (gi * 4 + zc) * NS + s
                        sc.wait_ge(s_zm, zi + 1)
                        if zg >= 2 and s == 0:
                            sc.wait_ge(st0 if zg % 2 == 0 else st1,
                                       (zg // 2) * 16)
                        sc.activation(zbuf[:, (zg % 2) * T + s * 512:(zg % 2) * T + (s + 1) * 512],
                                      ps_z[:, (zi % 2) * 512:(zi % 2) * 512 + 512],
                                      AF.Gelu).then_inc(s_ze, 1)


def emit_cproj(nc, P, gzin, wct, acc, out, T=2048, K=16384, O=1024):
    KC = K // 128
    NS = T // 512
    OC = O // 128
    gz_v = gzin.rearrange("(k p) t -> p k t", p=128)
    wct_v = wct.rearrange("(k p) o -> p k o", p=128)
    acc_v = acc.rearrange("(c p) t -> p c t", p=128)
    out_v = out.rearrange("(c p) t -> p c t", p=128)
    KG = 8
    NWD = KC // KG
    with ExitStack() as es:
        E = es.enter_context
        gzs = E(nc.sbuf_tensor(P + "gzs", [128, KC * 512], BF16))
        wcts = E(nc.sbuf_tensor(P + "wcts", [128, 2 * KG * O], BF16))
        accs = E(nc.sbuf_tensor(P + "accs", [128, OC * 512], F32))
        outb = E(nc.sbuf_tensor(P + "outb", [128, OC * 512], F32))
        ps = E(nc.psum_tensor(P + "ps", [128, OC * 512], F32))
        ldx = E(nc.semaphore(P + "ldx"))
        lda = E(nc.semaphore(P + "lda"))
        ldw = E(nc.semaphore(P + "ldw"))
        s_mm = E(nc.semaphore(P + "s_mm"))
        s_ev = E(nc.semaphore(P + "s_ev"))
        st = E(nc.semaphore(P + "st"))
        block = E(nc.Block())

        @block.sync
        def _(sy):
            for ts in range(NS):
                if ts > 0:
                    sy.wait_ge(s_mm, NWD * ts)
                for k0 in range(0, KC, KG):
                    sy.dma_start(gzs[:, k0 * 512:(k0 + KG) * 512]
                                 .rearrange("p (k t) -> p k t", t=512),
                                 gz_v[:, k0:k0 + KG, ts * 512:(ts + 1) * 512]
                                 ).then_inc(ldx, 16)

        @block.gpsimd
        def _(g):
            for ts in range(NS):
                if ts > 0:
                    g.wait_ge(s_ev, OC * ts)
                g.dma_start(accs[:, :].rearrange("p (c t) -> p c t", t=512),
                            acc_v[:, :, ts * 512:(ts + 1) * 512]).then_inc(lda, 16)
                for wi in range(NWD):
                    g.wait_ge(s_mm, max(0, ts * NWD + wi - 1))
                    g.dma_start(wcts[:, (wi % 2) * KG * O:((wi % 2) + 1) * KG * O]
                                .rearrange("p (k o) -> p k o", o=O),
                                wct_v[:, wi * KG:(wi + 1) * KG, :]).then_inc(ldw, 16)

        @block.tensor
        def _(t):
            for ts in range(NS):
                t.wait_ge(ldx, (ts + 1) * NWD * 16)
                if ts > 0:
                    t.wait_ge(s_ev, OC * ts)
                for wi in range(NWD):
                    t.wait_ge(ldw, (ts * NWD + wi + 1) * 16)
                    last = None
                    for kc in range(KG):
                        k = wi * KG + kc
                        for ot in range(OC):
                            last = t.matmul(
                                ps[:, ot * 512:(ot + 1) * 512],
                                wcts[:, (wi % 2) * KG * O + kc * O + ot * 128:
                                     (wi % 2) * KG * O + kc * O + (ot + 1) * 128],
                                gzs[:, k * 512:(k + 1) * 512],
                                start=(k == 0), stop=(k == KC - 1))
                    last.then_inc(s_mm, 1)

        @block.vector
        def _(v):
            for ts in range(NS):
                v.wait_ge(s_mm, (ts + 1) * NWD)
                v.wait_ge(lda, (ts + 1) * 16)
                if ts > 0:
                    v.wait_ge(st, ts * OC * 16)
                for ot in range(OC):
                    v.tensor_tensor(outb[:, ot * 512:(ot + 1) * 512],
                                    ps[:, ot * 512:(ot + 1) * 512],
                                    accs[:, ot * 512:(ot + 1) * 512],
                                    op=AluOpType.add).then_inc(s_ev, 1)

        @block.scalar
        def _(sc):
            for ts in range(NS):
                for ot in range(OC):
                    sc.wait_ge(s_ev, ts * OC + ot + 1)
                    sc.dma_start(out_v[:, ot, ts * 512:(ts + 1) * 512],
                                 outb[:, ot * 512:(ot + 1) * 512]).then_inc(st, 16)
            sc.wait_ge(st, NS * OC * 16)


def emit_railsproj(nc, P, xnt, rmc, ruc, wct, acc, out, T=2048):
    NS = T // 512
    K = 16384
    KC = K // 128
    wct_v = wct.rearrange("(k p) o -> p k o", p=128)
    acc_v = acc.rearrange("(c p) t -> p c t", p=128)
    out_v = out.rearrange("(c p) t -> p c t", p=128)
    xnt_v = xnt.rearrange("(c p) t -> p c t", p=128)
    WG = 4

    with ExitStack() as es:
        E = es.enter_context
        xsl = E(nc.sbuf_tensor(P + "xsl", [128, 8 * 512], BF16))
        rmcs = E(nc.sbuf_tensor(P + "rmcs", [128, 32 * 128], BF16))
        rucb = E(nc.sbuf_tensor(P + "rucb", [128, 2 * 32 * 128], BF16))
        ybuf = E(nc.sbuf_tensor(P + "ybuf", [128, 2 * 512], BF16))
        gzs = E(nc.sbuf_tensor(P + "gzs", [128, KC * 512], BF16))
        wcts = E(nc.sbuf_tensor(P + "wcts", [128, 4 * WG * 1024], BF16))
        accs = E(nc.sbuf_tensor(P + "accs", [128, 2 * 512], F32))
        oev = E(nc.sbuf_tensor(P + "oev", [128, 2 * 512], F32))
        ps = E(nc.psum_tensor(P + "ps", [128, 8 * 512], F32))
        ldx = E(nc.semaphore(P + "ldx"))
        ldr0 = E(nc.semaphore(P + "ldr0"))
        ldr1 = E(nc.semaphore(P + "ldr1"))
        ldm = E(nc.semaphore(P + "ldm"))
        ldw0 = E(nc.semaphore(P + "ldw0"))
        ldw1 = E(nc.semaphore(P + "ldw1"))
        ldw2 = E(nc.semaphore(P + "ldw2"))
        ldw3 = E(nc.semaphore(P + "ldw3"))
        lda0 = E(nc.semaphore(P + "lda0"))
        lda1 = E(nc.semaphore(P + "lda1"))
        s_ym = E(nc.semaphore(P + "s_ym"))
        s_ye = E(nc.semaphore(P + "s_ye"))
        s_zm = E(nc.semaphore(P + "s_zm"))
        s_ze = E(nc.semaphore(P + "s_ze"))
        s_cm = E(nc.semaphore(P + "s_cm"))
        s_cv = E(nc.semaphore(P + "s_cv"))
        st0 = E(nc.semaphore(P + "st0"))
        st1 = E(nc.semaphore(P + "st1"))
        block = E(nc.Block())

        @block.sync
        def _(sy):
            for ts in range(NS):
                if ts >= 1:
                    sy.wait_ge(s_ym, 32 * ts)
                sy.dma_start(xsl[:, :].rearrange("p (c t) -> p c t", t=512),
                             xnt_v[:, :, ts * 512:(ts + 1) * 512]).then_inc(ldx, 16)
                for ot in range(8):
                    cv = 8 * ts + ot
                    if cv >= 2:
                        sy.wait_ge(s_cv, cv - 1)
                    sy.dma_start(accs[:, (cv % 2) * 512:(cv % 2) * 512 + 512],
                                 acc_v[:, ot, ts * 512:(ts + 1) * 512]).then_inc(
                                     lda0 if cv % 2 == 0 else lda1, 16)

        @block.gpsimd
        def _(g):
            g.dma_start(rmcs[:, :].rearrange("p (g c) -> p g c", c=128),
                        rmc[:, :, :]).then_inc(ldm, 16)
            for ts in range(NS):
                for j in range(4):
                    jj = ts * 4 + j
                    if jj >= 2:
                        g.wait_ge(s_zm, 128 * ts + 32 * (j - 1) if j >= 1
                                  else 128 * (ts - 1) + 32 * 3)
                    g.dma_start(rucb[:, (jj % 2) * 32 * 128:((jj % 2) + 1) * 32 * 128]
                                .rearrange("p (g c) -> p g c", c=128),
                                ruc[:, j * 32:(j + 1) * 32, :]).then_inc(
                                    ldr0 if jj % 2 == 0 else ldr1, 16)
                for wi in range(KC // WG):
                    gwi = (KC // WG) * ts + wi
                    if gwi >= 4:
                        g.wait_ge(s_cm, WG * gwi - 3 * WG)
                    g.dma_start(wcts[:, (gwi % 4) * WG * 1024:((gwi % 4) + 1) * WG * 1024]
                                .rearrange("p (k o) -> p k o", o=1024),
                                wct_v[:, wi * WG:(wi + 1) * WG, :]).then_inc(
                                    [ldw0, ldw1, ldw2, ldw3][gwi % 4], 16)

        @block.tensor
        def _(t):
            t.wait_ge(ldm, 16)
            for ts in range(NS):
                t.wait_ge(ldx, (ts + 1) * 16)
                if ts > 0:
                    t.wait_ge(s_cv, 8 * ts)
                xo = 0

                def ymm(gi):
                    ro = gi % 8
                    yi = 32 * ts + gi
                    if yi >= 2:
                        t.wait_ge(s_ye, yi - 1)
                    t.matmul(ps[:, (gi % 2) * 512:(gi % 2) * 512 + 512],
                             rmcs[:, gi * 128:(gi + 1) * 128],
                             xsl[:, xo + ro * 512:xo + (ro + 1) * 512],
                             start=True, stop=True).then_inc(s_ym, 1)

                ymm(0)
                for gi in range(32):
                    yi = 32 * ts + gi
                    if gi + 1 < 32:
                        ymm(gi + 1)
                    for zc in range(4):
                        zl = gi * 4 + zc
                        zi = 128 * ts + zl
                        jj = ts * 4 + zl // 32
                        t.wait_ge(ldr0 if jj % 2 == 0 else ldr1,
                                  (jj // 2 + 1) * 16)
                        t.wait_ge(s_ye, yi + 1)
                        if zi >= 2:
                            t.wait_ge(s_ze, zi - 1)
                        t.matmul(ps[:, (2 + zl % 2) * 512:(2 + zl % 2) * 512 + 512],
                                 rucb[:, ((jj % 2) * 32 + zl % 32) * 128:
                                      ((jj % 2) * 32 + zl % 32 + 1) * 128],
                                 ybuf[:, (gi % 2) * 512:(gi % 2) * 512 + 512],
                                 start=True, stop=True).then_inc(s_zm, 1)
                t.wait_ge(s_ze, 128 * (ts + 1))
                t.wait_ge(s_ye, 32 * (ts + 1))
                for zc in range(KC):
                    wi = zc // WG
                    gwi = (KC // WG) * ts + wi
                    t.wait_ge([ldw0, ldw1, ldw2, ldw3][gwi % 4],
                              (gwi // 4 + 1) * 16)
                    wof = (gwi % 4) * WG * 1024 + (zc % WG) * 1024
                    last = None
                    for ot in range(8):
                        last = t.matmul(
                            ps[:, ot * 512:(ot + 1) * 512],
                            wcts[:, wof + ot * 128:wof + (ot + 1) * 128],
                            gzs[:, zc * 512:(zc + 1) * 512],
                            start=(zc == 0), stop=(zc == KC - 1))
                    last.then_inc(s_cm, 1)

        @block.vector
        def _(v):
            for ts in range(NS):
                for gi in range(32):
                    yi = 32 * ts + gi
                    v.wait_ge(s_ym, yi + 1)
                    if yi >= 2:
                        v.wait_ge(s_zm, 4 * (yi - 1))
                    v.tensor_copy(ybuf[:, (gi % 2) * 512:(gi % 2) * 512 + 512],
                                  ps[:, (gi % 2) * 512:(gi % 2) * 512 + 512]
                                  ).then_inc(s_ye, 1)
                v.wait_ge(s_cm, 128 * (ts + 1))
                for ot in range(8):
                    cv = 8 * ts + ot
                    v.wait_ge(lda0 if cv % 2 == 0 else lda1, (cv // 2 + 1) * 16)
                    if cv >= 2:
                        v.wait_ge(st0 if cv % 2 == 0 else st1, (cv // 2) * 16)
                    v.tensor_tensor(oev[:, (cv % 2) * 512:(cv % 2) * 512 + 512],
                                    ps[:, ot * 512:(ot + 1) * 512],
                                    accs[:, (cv % 2) * 512:(cv % 2) * 512 + 512],
                                    op=AluOpType.add).then_inc(s_cv, 1)

        @block.scalar
        def _(sc):
            for ts in range(NS):
                for zl in range(128):
                    zi = 128 * ts + zl
                    sc.wait_ge(s_zm, zi + 1)
                    if ts > 0 and zl == 0:
                        sc.wait_ge(s_cm, 128 * ts)
                    sc.activation(gzs[:, zl * 512:(zl + 1) * 512],
                                  ps[:, (2 + zl % 2) * 512:(2 + zl % 2) * 512 + 512],
                                  AF.Gelu).then_inc(s_ze, 1)
                for ot in range(8):
                    cv = 8 * ts + ot
                    sc.wait_ge(s_cv, cv + 1)
                    sc.dma_start(out_v[:, ot, ts * 512:(ts + 1) * 512],
                                 oev[:, (cv % 2) * 512:(cv % 2) * 512 + 512]
                                 ).then_inc(st0 if cv % 2 == 0 else st1, 16)
            sc.wait_ge(st0, NS * 4 * 16)
            sc.wait_ge(st1, NS * 4 * 16)


def emit_glu(nc, P, xt, w1, w2, s1c, s2c, out, T=2048, K=1024, O=4096):
    KC, TS = K // 128, T // 512
    OB = O // 512
    OC = O // 128
    NT = OB * TS * 4
    xt_v = xt.rearrange("(k p) t -> p k t", p=128)
    w1_v = w1.rearrange("(k p) o -> p k o", p=128)
    w2_v = w2.rearrange("(k p) o -> p k o", p=128)
    with ExitStack() as es:
        E = es.enter_context
        xblk = E(nc.sbuf_tensor(P + "xblk", [128, KC * T], BF16))
        wb1 = E(nc.sbuf_tensor(P + "wb1", [128, 2 * KC * 512], BF16))
        wb2 = E(nc.sbuf_tensor(P + "wb2", [128, 2 * KC * 512], BF16))
        tmp = E(nc.sbuf_tensor(P + "tmp", [128, 2 * 512], BF16))
        evac = E(nc.sbuf_tensor(P + "evac", [128, 2 * 512], BF16))
        col1 = E(nc.sbuf_tensor(P + "col1", [128, OC], F32))
        col2 = E(nc.sbuf_tensor(P + "col2", [128, OC], F32))
        ps = E(nc.psum_tensor(P + "ps", [128, 4 * 512], F32))
        ldx = E(nc.semaphore(P + "ldx"))
        ldw0 = E(nc.semaphore(P + "ldw0"))
        ldw1 = E(nc.semaphore(P + "ldw1"))
        lcol = E(nc.semaphore(P + "lcol"))
        m1 = E(nc.semaphore(P + "m1"))
        mm = E(nc.semaphore(P + "mm"))
        sil = E(nc.semaphore(P + "sil"))
        ev = E(nc.semaphore(P + "ev"))
        st0 = E(nc.semaphore(P + "st0"))
        st1 = E(nc.semaphore(P + "st1"))
        block = E(nc.Block())

        @block.gpsimd
        def _(gp):
            gp.dma_start(col1[:, :], s1c).then_inc(lcol, 16)
            gp.dma_start(col2[:, :], s2c).then_inc(lcol, 16)
            for obi in range(OB):
                if obi >= 2:
                    gp.wait_ge(mm, (obi - 1) * TS * 4)
                po = (obi % 2) * KC * 512
                gp.dma_start(wb1[:, po:po + KC * 512]
                             .rearrange("p (k c) -> p k c", c=512),
                             w1_v[:, :, obi * 512:(obi + 1) * 512]).then_inc(
                                 ldw0 if obi % 2 == 0 else ldw1, 16)
                gp.dma_start(wb2[:, po:po + KC * 512]
                             .rearrange("p (k c) -> p k c", c=512),
                             w2_v[:, :, obi * 512:(obi + 1) * 512]).then_inc(
                                 ldw0 if obi % 2 == 0 else ldw1, 16)

        @block.sync
        def _(sy):
            sy.dma_start(xblk[:, :].rearrange("p (k c) -> p k c", c=T),
                         xt_v[:, :, :]).then_inc(ldx, 16)

        @block.tensor
        def _(t):
            t.wait_ge(ldx, 16)
            for obi in range(OB):
                t.wait_ge(ldw0 if obi % 2 == 0 else ldw1,
                          (obi // 2 + 1) * 32)
                po = (obi % 2) * KC * 512
                for ti in range(TS):
                    for oi in range(4):
                        g = (obi * TS + ti) * 4 + oi
                        b = (g % 2) * 2
                        if g >= 2:
                            t.wait_ge(sil, g - 1)
                            t.wait_ge(ev, g - 1)
                        last = None
                        for k in range(KC):
                            last = t.matmul(
                                ps[:, b * 512:(b + 1) * 512],
                                wb1[:, po + k * 512 + oi * 128:
                                    po + k * 512 + (oi + 1) * 128],
                                xblk[:, k * T + ti * 512:k * T + (ti + 1) * 512],
                                start=(k == 0), stop=(k == KC - 1))
                        last.then_inc(m1, 1)
                        for k in range(KC):
                            last = t.matmul(
                                ps[:, (b + 1) * 512:(b + 2) * 512],
                                wb2[:, po + k * 512 + oi * 128:
                                    po + k * 512 + (oi + 1) * 128],
                                xblk[:, k * T + ti * 512:k * T + (ti + 1) * 512],
                                start=(k == 0), stop=(k == KC - 1))
                        last.then_inc(mm, 1)

        @block.scalar
        def _(sc):
            sc.wait_ge(lcol, 32)
            for g in range(NT):
                o = (g // (TS * 4)) * 4 + g % 4
                b = (g % 2) * 2
                sc.wait_ge(m1, g + 1)
                if g >= 2:
                    sc.wait_ge(ev, g - 1)
                sc.activation(tmp[:, (g % 2) * 512:(g % 2) * 512 + 512],
                              ps[:, b * 512:(b + 1) * 512], AF.Silu,
                              scale=col1[:, o:o + 1]).then_inc(sil, 1)
                sc.wait_ge(ev, g + 1)
                oi, it = g % 4, g // 4
                ti = it % TS
                sc.dma_start(out[o * 128:(o + 1) * 128, ti * 512:(ti + 1) * 512],
                             evac[:, (g % 2) * 512:(g % 2) * 512 + 512]
                             ).then_inc(st0 if g % 2 == 0 else st1, 16)
            sc.wait_ge(st0, ((NT + 1) // 2) * 16)
            sc.wait_ge(st1, (NT // 2) * 16)

        @block.vector
        def _(v):
            for g in range(NT):
                o = (g // (TS * 4)) * 4 + g % 4
                b = (g % 2) * 2
                v.wait_ge(mm, g + 1)
                v.wait_ge(sil, g + 1)
                if g >= 2:
                    v.wait_ge(st0 if g % 2 == 0 else st1, (g // 2) * 16)
                v.scalar_tensor_tensor(
                    evac[:, (g % 2) * 512:(g % 2) * 512 + 512],
                    ps[:, (b + 1) * 512:(b + 2) * 512], col2[:, o:o + 1],
                    tmp[:, (g % 2) * 512:(g % 2) * 512 + 512],
                    op0=AluOpType.mult, op1=AluOpType.mult).then_inc(ev, 1)


def build_fused(T=2048):
    nc = bass.Bass("TRN2", target_bir_lowering=False, debug=False)
    ei = lambda n, s, d: nc.dram_tensor(n, s, d, kind="ExternalInput").ap()
    sc = lambda n, s, d: nc.dram_tensor(n, s, d).ap()

    rt = ei("rt", [1024, T], F32)
    qkw0 = ei("qkw0", [1024, 2048], F32)
    wv0 = ei("wv0", [1024, 1024], F32)
    wo0 = ei("wo0", [1024, 1024], F32)
    bo0 = ei("bo0", [128, 8], F32)
    masks = ei("masks", [128, 4, 512], BF16)
    rmc = ei("rmc", [128, 32, 128], F32)
    ruc = ei("ruc", [128, 128, 128], F32)
    wct = ei("wct", [16384, 1024], F32)
    qkw1 = ei("qkw1", [1024, 2048], F32)
    wv1 = ei("wv1", [1024, 1024], F32)
    wo1 = ei("wo1", [1024, 1024], F32)
    nw1 = ei("nw1", [128, 8], F32)
    bo1 = ei("bo1", [128, 8], F32)
    dw = ei("dw", [1, 1], F32)
    w1t = ei("w1t", [1024, 4096], F32)
    w2t = ei("w2t", [1024, 4096], F32)
    h1t = ei("h1t", [4096, 4096], F32)
    h2t = ei("h2t", [4096, 4096], F32)
    w3t = ei("w3t", [4096, 1024], F32)
    s1c = ei("s1c", [128, 32], F32)
    s2c = ei("s2c", [128, 32], F32)
    hs1c = ei("hs1c", [128, 32], F32)
    hs2c = ei("hs2c", [128, 32], F32)
    s3c = ei("s3c", [128, 8], F32)

    xnt = sc("xnt", [1024, T], BF16)
    qk = sc("qk", [2048, T], BF16)
    vp = sc("vp", [128, 16, T // 128, 65], BF16)
    vh = sc("vh", [128, 8, T // 128, 128], BF16)
    zts = sc("ztsc", [1024, T], BF16)
    rta = sc("rta", [1024, T], F32)
    gz = sc("gz", [16384, T], BF16)
    rt1 = sc("rt1", [1024, T], F32)
    ntc = sc("ntc", [1024, T], BF16)
    bec = sc("bec", [128, 8], F32)
    rtb = sc("rtb", [1024, T], F32)
    aT = sc("aT", [4096, T], BF16)
    g1 = sc("g1", [4096, T], BF16)
    g2 = sc("g2", [4096, T], BF16)
    g3 = sc("g3", [4096, T], BF16)
    out = nc.dram_tensor("out", [1024, T], F32, kind="ExternalOutput").ap()

    phases = [
        lambda: emit_ln(nc, "a_", rt, xnt, 1024, T),
        lambda: emit_vtok(nc, "c_", xnt, wv0, vp, T, 64),
        lambda: emit_attn0(nc, "d_", xnt, qkw0, vp, masks, zts, T, 16),
        lambda: emit_mm(nc, "e_", zts, wo0, rta, 1024, 1024, T, colv=bo0,
                        acc=rt, obias=True, out_f32=True),
        lambda: emit_railsproj(nc, "f_", xnt, rmc, ruc, wct, rta, rt1, T),
        lambda: emit_ln(nc, "h_", rt1, xnt, 1024, T),
        lambda: emit_mm(nc, "i_", xnt, qkw1, qk, 1024, 2048, T),
        lambda: emit_vtok(nc, "j_", xnt, wv1, vh, T, 128),
        lambda: emit_attn1(nc, "k_", qk[0:1024, :], qk[1024:2048, :], vh,
                           dw[:, :], zts, T, 8),
        lambda: emit_rms(nc, "l_", zts, nw1[:, :], bo1[:, :], dw[:, :], ntc,
                         bec[:, :], T, 1024),
        lambda: emit_mm(nc, "m_", ntc, wo1, rtb, 1024, 1024, T, colv=bec[:, :],
                        acc=rt1, obias=True, out_f32=True),
        lambda: emit_glu(nc, "n_", xnt, w1t, w2t, s1c[:, :], s2c[:, :], g1, T),
        lambda: emit_mm(nc, "p_", g1, h1t, g2, 4096, 4096, T, colv=hs1c[:, :],
                        act=AF.Gelu, oscale=True),
        lambda: emit_mm(nc, "q_", g2, h2t, g3, 4096, 4096, T, colv=hs2c[:, :],
                        act=AF.Gelu, oscale=True),
        lambda: emit_mm(nc, "r_", g3, w3t, out, 4096, 1024, T, colv=s3c[:, :],
                        acc=rtb, oscale=True, out_f32=True),
    ]
    for ph in phases:
        with nc.reset_on_exit():
            ph()
    return nc


def host_inputs(inp):
    f32 = np.float32
    C = np.ascontiguousarray
    A = np.asarray
    qkw0 = C(np.concatenate([A(inp["L0_attn_Wq"]).T, A(inp["L0_attn_Wk"]).T],
                            axis=1)).astype(f32)
    wv0 = C(A(inp["L0_attn_Wv"]).T.astype(f32))
    wo0 = C(A(inp["L0_attn_Wo"]).T.astype(f32))
    bo0 = C(A(inp["L0_attn_bo"]).astype(f32).reshape(8, 128).T)
    rm = A(inp["L0_ffn_rail_maker"]).astype(f32)
    ru = A(inp["L0_ffn_rail_up_proj"]).astype(f32)
    rmc = np.zeros((128, 32, 128), f32)
    ruc = np.zeros((128, 128, 128), f32)
    for h in range(4):
        for ro in range(8):
            gi = h * 8 + ro
            for rr in range(8):
                rmc[16*rr:16*rr+16, gi, 16*rr:16*rr+16] = rm[h, 8*ro+rr]
            for zc in range(4):
                for d in range(2):
                    ruc[16*(2*zc+d):16*(2*zc+d)+16, gi*4+zc, 64*d:64*d+64] = \
                        ru[h, 8*ro+2*zc+d].T
    wct = C(A(inp["L0_ffn_w_cproj"]).T.astype(f32))
    qkw1 = C(np.concatenate([A(inp["L1_attn_Wq"]).T, A(inp["L1_attn_Wk"]).T],
                            axis=1)).astype(f32)
    wv1 = C(A(inp["L1_attn_Wv"]).T.astype(f32))
    wo1 = C(A(inp["L1_attn_Wo"]).T.astype(f32))
    nw1 = C(A(inp["L1_attn_norm_w"]).astype(f32).reshape(8, 128).T)
    bo1 = C(A(inp["L1_attn_bo"]).astype(f32).reshape(8, 128).T)
    dw = A(inp["L1_attn_diff_weight"]).astype(f32).reshape(1, 1)
    w1t = C(A(inp["L1_ffn_w1_w"]).T.astype(f32))
    w2t = C(A(inp["L1_ffn_w2_w"]).T.astype(f32))
    h1t = C(A(inp["L1_ffn_h1_w"]).T.astype(f32))
    h2t = C(A(inp["L1_ffn_h2_w"]).T.astype(f32))
    w3t = C(A(inp["L1_ffn_w3_w"]).T.astype(f32))
    s1c = C(A(inp["L1_ffn_w1_s"]).astype(f32).reshape(32, 128).T)
    s2c = C(A(inp["L1_ffn_w2_s"]).astype(f32).reshape(32, 128).T)
    hs1c = C(A(inp["L1_ffn_h1_s"]).astype(f32).reshape(32, 128).T)
    hs2c = C(A(inp["L1_ffn_h2_s"]).astype(f32).reshape(32, 128).T)
    s3c = C(A(inp["L1_ffn_w3_s"]).astype(f32).reshape(8, 128).T)
    r = np.arange(128)[:, None]
    c = np.arange(512)[None, :]
    m = np.zeros((128, 4, 512), f32)
    for j in range(4):
        blk = c // 128
        m[:, j, :] = ((blk > j) | ((blk == j) & (r <= c % 128))).astype(f32)
    masks = m.astype(BF)
    shared = dict(qkw0=qkw0, wv0=wv0, wo0=wo0, bo0=bo0, masks=masks, rmc=rmc,
                  ruc=ruc, wct=wct, qkw1=qkw1, wv1=wv1, wo1=wo1, nw1=nw1,
                  bo1=bo1, dw=dw, w1t=w1t, w2t=w2t, h1t=h1t, h2t=h2t, w3t=w3t,
                  s1c=s1c, s2c=s2c, hs1c=hs1c, hs2c=hs2c, s3c=s3c)
    x = A(inp["L0_attn_batch"]).astype(f32)
    maps = []
    for cc in range(x.shape[0]):
        m2 = dict(shared)
        m2["rt"] = C(x[cc].T)
        maps.append(m2)
    return maps


def kernel(**inp):
    maps = host_inputs(inp)
    nc = build_fused(2048)
    res = run_bass_kernel_spmd(nc, maps, core_ids=list(range(len(maps))))
    return np.stack([r["out"].T for r in res.results], axis=0)
